# version 09
# baseline (speedup 1.0000x reference)
import sys
sys.path.insert(0, "/opt/trn_rl_repo")
import contextlib
import numpy as np
import ml_dtypes

import concourse.bass as bass
import concourse.mybir as mybir
from concourse.bass import AP
from concourse.bass_utils import run_bass_kernel_spmd

F32 = mybir.dt.float32
BF16 = mybir.dt.bfloat16
I32 = mybir.dt.int32
AF = mybir.ActivationFunctionType
ALU = mybir.AluOpType
AXX = mybir.AxisListType.X
NPBF = ml_dtypes.bfloat16

P = 128
L = 4096
DM = 1024
DI = 2048
NH = 32
HD = 64
CD = 2304
DIP = 4384
Q = 128
NST = L // Q
FST = 256
NFST = L // FST
DH = 4096
NEGBIG = -1.0e5

_DBG = {}

class _StopRec(Exception):
    pass


class Buf:
    __slots__ = ("name", "writer", "readers")
    def __init__(self, name):
        self.name = name
        self.writer = None
        self.readers = []


class Op:
    __slots__ = ("idx", "engine", "fn", "deps", "token", "needs_inc", "is_dma",
                 "selfdep")
    def __init__(self, idx, engine, fn, is_dma):
        self.idx = idx
        self.engine = engine
        self.fn = fn
        self.deps = []
        self.token = None
        self.needs_inc = False
        self.is_dma = is_dma
        self.selfdep = False


class Prog:
    def __init__(self):
        self.ops = []
        self.bufs = {}

    def buf(self, name):
        if name not in self.bufs:
            self.bufs[name] = Buf(name)
        return self.bufs[name]

    def add(self, engine, fn, reads, writes, is_dma=False, token=None,
            selfdep=False):
        op = Op(len(self.ops), engine, fn, is_dma)
        op.token = token
        op.selfdep = selfdep
        rbufs = [self.buf(n) for n in reads]
        wbufs = [self.buf(n) for n in writes]
        for b in rbufs:
            if b.writer is not None:
                op.deps.append(b.writer)
        for b in wbufs:
            if b.writer is not None:
                op.deps.append(b.writer)
            op.deps.extend(b.readers)
        for b in wbufs:
            b.writer = op
            b.readers = []
        for b in rbufs:
            b.readers.append(op)
        self.ops.append(op)
        return op


def host_consts():
    r = np.arange(Q)
    c = {}
    for d, fwd in (("f", True), ("b", False)):
        tri_incl = (r[:, None] <= r[None, :]) if fwd else (r[:, None] >= r[None, :])
        tri_strict = (r[:, None] > r[None, :]) if fwd else (r[:, None] < r[None, :])
        mask_l = (r[:, None] < r[None, :]) if fwd else (r[:, None] > r[None, :])
        c[f"tri_incl_{d}"] = tri_incl.astype(NPBF)
        c[f"tri_strict_{d}"] = tri_strict.astype(NPBF)
    c["ident16"] = np.eye(P, dtype=NPBF)
    c["ident32"] = np.eye(P, dtype=np.float32)
    c["ones_bf"] = np.ones((P, P), dtype=NPBF)
    c["idneg"] = (np.eye(P, dtype=np.float32) * NEGBIG).astype(NPBF)
    c["cmisc"] = np.broadcast_to(
        np.array([1.0, 1e-5, -1.0, -0.5, 0.5, 0.7], np.float32), (P, 6)).copy()
    return c



def dst_half(dst, hh2):
    return AP(dst.tensor, dst.offset + hh2 * 1024,
              [list(dst.ap[0]), [1, 1024]])

def build_program():
    nc = bass.Bass("TRN2", target_bir_lowering=False,
                   detect_race_conditions=False)
    pr = Prog()
    es = contextlib.ExitStack()

    D = {}
    def din(name, shape, dtype=F32):
        D[name] = nc.dram_tensor(name, list(shape), dtype, kind="ExternalInput").ap()
    din("u", (L, DM))
    for d in ("f", "b"):
        din(f"wiT_{d}", (DM, DIP))
        din(f"woT_{d}", (DI, DM))
        din(f"convw_{d}", (P, 72))
        din(f"convb_{d}", (P, 18))
        din(f"dtb_{d}", (P, NH))
        din(f"alog_{d}", (P, NH))
        din(f"dvec_{d}", (P, NH))
        din(f"normw_{d}", (P, 16))
    din("w1T", (DM, DH))
    din("w2T", (DH, DM))
    din("fconvw", (P, 96))
    din("fconvb", (P, 32))
    HC = host_consts()
    for k, v in HC.items():
        din("c_" + k, v.shape, BF16 if v.dtype == NPBF else F32)
    out_d = nc.dram_tensor("out", [L, DM], F32, kind="ExternalOutput").ap()
    outf_d = nc.dram_tensor("scr_outf", [L, DM], F32, kind="Internal").ap()
    x1_d = nc.dram_tensor("scr_x1", [L, DM], F32, kind="Internal").ap()
    w1q_d = nc.dram_tensor("scr_w1q", [DM, DH], BF16, kind="Internal").ap()
    w2q_d = nc.dram_tensor("scr_w2q", [DH, DM], BF16, kind="Internal").ap()
    dbg_tensors = {}

    TI = {}
    def sb(name, shape, dtype):
        TI[name] = es.enter_context(nc.sbuf_tensor(name, shape, dtype))
        return TView(TI[name], 0, shape[1])

    class TView:
        __slots__ = ("t", "col0", "ncols")
        def __init__(self, t, col0, ncols):
            self.t = t; self.col0 = col0; self.ncols = ncols
        def __getitem__(self, idx):
            if idx == slice(None):
                return self.t[:, self.col0:self.col0 + self.ncols]
            if isinstance(idx, tuple):
                ps, cs = idx
                c0 = self.col0 + (cs.start or 0)
                c1 = self.col0 + (cs.stop if cs.stop is not None else self.ncols)
                return self.t[ps, c0:c1]
            raise KeyError(idx)
        def view(self, col0, ncols):
            return TView(self.t, self.col0 + col0, ncols)

    globals()["TView"] = TView

    def sb(name, shape, dtype):
        TI[name] = es.enter_context(nc.sbuf_tensor(name, shape, dtype))
        return TView(TI[name], 0, shape[1])

    wbig = sb("wbig", [P, 8 * DIP], BF16)
    wsec = sb("wsec", [P, 32 * DM], BF16)
    h1t = sb("h1t", [P, 32 * (FST + 2)], BF16)
    h2t = sb("h2t", [P, 32 * FST], BF16)
    h0 = sb("h0", [P, DI], F32)
    h0b = sb("h0b", [P, DI], BF16)
    uu = sb("uu", [P, DI], BF16)
    uf32 = sb("uf32", [P, DM], F32)
    comb = sb("comb", [P, DM], F32)
    wstage = sb("wstage", [P, 2048], F32)
    g1t = sb("g1t", [P, 1024], BF16)
    g2t = sb("g2t", [P, 1024], BF16)
    iqc = sb("iqc", [P, 256], I32)
    u16 = uu.view(0, DM)
    uT = uu.view(DM, DM)
    zsil = h1t.view(0, DI)
    xprime = h1t.view(DI, DI)
    xpp = h1t.view(2 * DI, DI)
    xdt = h1t.view(3 * DI, DI)
    xTs = h2t.view(0, DI)
    y0t = h2t.view(DI, DI)
    t1t = h2t.view(2 * DI, DI)
    ynt = h2t.view(3 * DI, DI)
    wg1 = wsec.view(16384, 2048)
    wg2 = wsec.view(16384 + 2048, 2048)
    rhsa = wsec.view(16384 + 4096, 2048)
    yTt = wsec.view(16384 + 6144, 2048)
    xbcc = wsec.view(16384 + 8192, 18 * Q)
    xbcpre = wsec.view(16384 + 8192 + 2304, 18 * (Q + 3))
    xbctm = wsec.view(16384 + 8192 + 2304 + 2358, CD)
    btok = wsec.view(16384 + 8192 + 2304 + 2358 + 2304, Q)
    x1t = h0.view(0, DI)
    q1b = h0b.view(0, DI)
    q1hb = g2t.view(0, DM)
    q1T = wbig.view(32768, 8 * (FST + 1))
    sgq = uu.view(DM, DM)
    macc = uu.view(0, FST)
    mincc = uu.view(FST, FST)
    sgnc = uu.view(2 * FST, FST)
    tqc = wstage.view(0, FST)
    s2B = wstage.view(1280, FST)
    s2rep = wstage.view(1536, P)
    qstage = uf32.view(0, 512)
    ftmp = wstage.view(1664, P)
    mrowT = wstage.view(1792, P)
    dtraw = sb("dtraw", [P, NH], F32)
    dtrawB = sb("dtrawB", [P, NH], F32)
    sm1 = sb("sm1", [P, NH], F32)
    sm2 = sb("sm2", [P, NH], F32)
    sm3 = sb("sm3", [P, NH], F32)
    dtf = sb("dtf", [P, NH], F32)
    a16t = sb("a16t", [P, NH], BF16)
    p16 = sb("p16", [P, NH], BF16)
    expac = sb("expac", [P, NH], F32)
    dtrv = sb("dtrv", [P, NH], F32)
    cdb = sb("cdb", [P, NH], F32)
    stat = sb("stat", [P, 8], F32)
    aexp = {d: sb(f"aexp{d}", [P, NH], F32) for d in ("f", "b")}
    dtbS = {d: sb(f"dtbS{d}", [P, NH], F32) for d in ("f", "b")}
    dvecS = {d: sb(f"dvecS{d}", [P, NH], F32) for d in ("f", "b")}
    cwS = {d: sb(f"cwS{d}", [P, 72], F32) for d in ("f", "b")}
    cbS = {d: sb(f"cbS{d}", [P, 18], F32) for d in ("f", "b")}
    nwS = {d: sb(f"nwS{d}", [P, 16], F32) for d in ("f", "b")}
    CT = {}
    for k, v in HC.items():
        CT[k] = sb("k_" + k, list(v.shape), BF16 if v.dtype == NPBF else F32)
    beta = sb("beta", [P, 1], F32)
    zcol = sb("zcol", [P, 1], F32)
    nwh = sb("nwh", [P, 16], F32)
    i32s = sb("i32s", [P, 1], I32)
    s2all = sb("s2all", [P, 6], F32)
    s2c = s2all.view(0, 2)
    s2cl = s2all.view(2, 2)
    f2c = s2all.view(4, 2)
    fcwS = sb("fcwS", [P, 96], F32)
    fcbS = sb("fcbS", [P, 32], F32)
    msum = sb("msum", [P, 64], F32)
    mtot = sb("mtot", [P, 4], F32)
    sq1 = sb("sq1", [P, 6], F32)

    PB = es.enter_context(nc.psum_tensor("PB", [P, 2048], F32))
    PX = es.enter_context(nc.psum_tensor("PX", [P, 2048], BF16))
    PA = es.enter_context(nc.psum_tensor("PA", [P, 512], F32))
    PG = es.enter_context(nc.psum_tensor("PG", [P, 512], F32))
    PB_BANKS = ["PB0", "PB1", "PB2", "PB3"]

    def pbbank(col):
        return f"PB{col // 512}"

    sems = {}
    for n in ("c_tensor", "c_vector", "c_scalar", "c_gpsimd"):
        sems[n] = es.enter_context(nc.semaphore(n))
    NPOOL = 8
    dma_state = {"sync": [], "gpsimd": []}
    for qn in ("sync", "gpsimd"):
        for i in range(NPOOL):
            name = f"dq_{qn}_{i}"
            sems[name] = es.enter_context(nc.semaphore(name))
            dma_state[qn].append({"name": name, "count": 0})
    dma_rr = {"sync": 0, "gpsimd": 0}

    def op(engine, fn, reads=(), writes=(), selfdep=False):
        return pr.add(engine, fn, reads, writes, selfdep=selfdep)

    def dma(queue, out_ap, in_ap, reads=(), writes=()):
        slot = dma_state[queue][dma_rr[queue] % NPOOL]
        dma_rr[queue] += 1
        prev = slot["count"]
        slot["count"] = prev + 16
        name = slot["name"]
        def fn(eng, out_ap=out_ap, in_ap=in_ap, prev=prev, name=name):
            if prev > 0:
                eng.wait_ge(sems[name], prev)
            return eng.dma_start(out_ap, in_ap).then_inc(sems[name], 16)
        return pr.add(queue, fn, reads, writes, is_dma=True,
                      token=(name, prev + 16))

    def dbg_point(tag, tiles):
        if _DBG.get("stop") != tag:
            return
        for nm, (tv, bufn, dt_, ncols, nparts) in tiles.items():
            t = nc.dram_tensor("dbg_" + nm, [nparts, ncols], dt_,
                               kind="ExternalOutput").ap()
            dma(SY, t, tv[0:nparts, slice(0, ncols)], [bufn], [])
        raise _StopRec()

    def E_mm(out_ap, lhsT_ap, rhs_ap, start, stop):
        return lambda e: e.matmul(out_ap, lhsT_ap, rhs_ap, start=start, stop=stop)
    def E_tr(out_ap, in_ap, ident):
        return lambda e: e.transpose(out_ap, in_ap, ident)
    def E_trg(out_ap, in_ap, ident, start, stop):
        return lambda e: e.matmul(out_ap, in_ap, ident, start=start, stop=stop,
                                  is_transpose=True)
    def E_tt(out_ap, a, b, opx):
        return lambda e: e.tensor_tensor(out_ap, a, b, opx)
    def E_ts(out_ap, in_ap, s1, s2, op0, op1=None):
        if op1 is None:
            return lambda e: e.tensor_scalar(out_ap, in_ap, s1, None, op0)
        return lambda e: e.tensor_scalar(out_ap, in_ap, s1, s2, op0, op1)
    def E_stt(out_ap, in0, sc, in1, op0, op1):
        return lambda e: e.scalar_tensor_tensor(out_ap, in0, sc, in1, op0, op1)
    def E_cp(out_ap, in_ap):
        return lambda e: e.tensor_copy(out_ap, in_ap)
    def E_act(out_ap, in_ap, func, bias=None, scale=None, accum=None):
        kw = {}
        if bias is not None: kw["bias"] = bias
        if scale is not None: kw["scale"] = scale
        if accum is not None: kw["accum_out"] = accum
        return lambda e: e.activation(out_ap, in_ap, func, **kw)
    def E_red(out_ap, in_ap, opx, absv=False):
        return lambda e: e.tensor_reduce(out_ap, in_ap, AXX, opx,
                                         apply_absolute_value=absv)
    def E_memset(ap_, v):
        return lambda e: e.memset(ap_, v)
    def E_recip(out_ap, in_ap):
        return lambda e: e.reciprocal(out_ap, in_ap)

    def bap(tile, free_dims, offset=0, pcount=P, pstart=0):
        base = tile[:]
        pstep = base.ap[0][0]
        return AP(base.tensor, base.offset + offset + pstart * pstep,
                  [[pstep, pcount]] + [list(x) for x in free_dims])

    def r3(tile_ap, a):
        return tile_ap.rearrange("p (a b) -> p a b", a=a)

    def zap(n, pc=P):
        return bap(zcol, [(0, n)], pcount=pc)

    V, S, T, G, SY = "vector", "scalar", "tensor", "gpsimd", "sync"

    for k in HC:
        dma(SY, CT[k][:], D["c_" + k], [], ["k_" + k])
    for d in ("f", "b"):
        dma(SY, cwS[d][:], D[f"convw_{d}"], [], [f"cw{d}"])
        dma(SY, cbS[d][:], D[f"convb_{d}"], [], [f"cb{d}"])
        dma(SY, dtbS[d][:], D[f"dtb_{d}"], [], [f"dtb{d}"])
        dma(SY, dvecS[d][:], D[f"dvec_{d}"], [], [f"dv{d}"])
        dma(SY, aexp[d][:], D[f"alog_{d}"], [], [f"al{d}"])
        dma(SY, nwS[d][:], D[f"normw_{d}"], [], [f"nw{d}"])
    dma(SY, fcwS[:], D["fconvw"], [], ["fcw"])
    dma(SY, fcbS[:], D["fconvb"], [], ["fcb"])
    op(G, E_memset(TI["zcol"][:], 0.0), [], ["zcol"])
    op(V, E_cp(i32s[:], CT["cmisc"][:, 5:6]), ["k_cmisc"], ["i32s"])
    op(V, E_stt(beta[:], i32s[:], -0.5, CT["cmisc"][:, 4:5], ALU.mult, ALU.add),
       ["i32s", "k_cmisc"], ["beta"], selfdep=True)
    for d in ("f", "b"):
        op(S, E_act(aexp[d][:], aexp[d][:], AF.Exp), [f"al{d}"], [f"al{d}"])

    def load_wi(d):
        dma(G, r3(wbig[:], 8), D[f"wiT_{d}"].rearrange("(j p) c -> p j c", p=P),
            [], ["wbig"])

    def load_wo(d):
        dma(G, AP(wsec[:].tensor, 0, [[32 * DM, P], [DM, 16], [1, DM]]),
            D[f"woT_{d}"].rearrange("(j p) c -> p j c", p=P), [], ["wsec"])
        op(V, E_stt(nwh[:], nwS[d][:], 0.5, zap(16), ALU.mult, ALU.add),
           [f"nw{d}", "zcol"], ["nwh"])
        for j in range(16):
            op(S, E_act(wsec[:, j * DM:(j + 1) * DM], wsec[:, j * DM:(j + 1) * DM],
                        AF.Copy, scale=nwh[:, j:j + 1]),
               ["wsec", "nwh"], ["wsec"])

    def wprep_abs(i):
        if i < 16:
            j, h = i // 2, i % 2
            src = D["w1T"][j * P:(j + 1) * P, h * 2048:(h + 1) * 2048]
            dma(SY, wstage[:], src, [], ["wstage"])
            op(S, E_act(g2t[:], wstage[:, 0:DM], AF.Abs, accum=msum[:, i:i + 1]),
               ["wstage"], ["g2t", "msum"])
            op(S, E_act(g2t[:], wstage[:, DM:2 * DM], AF.Abs,
                        accum=msum[:, 16 + i:17 + i]),
               ["wstage"], ["g2t", "msum"])
        else:
            for kk in range(2):
                jb = (i - 16) * 2 + kk
                src = D["w2T"][jb * P:(jb + 1) * P, :]
                dma(SY, wstage[:, 0:DM], src, [], ["wstage"])
                op(S, E_act(g2t[:], wstage[:, 0:DM], AF.Abs,
                            accum=msum[:, 32 + jb:33 + jb]),
                   ["wstage"], ["g2t", "msum"])

    def wprep_tern(i):
        if i < 16:
            j, h = i // 2, i % 2
            src = D["w1T"][j * P:(j + 1) * P, h * 2048:(h + 1) * 2048]
            dst = w1q_d[j * P:(j + 1) * P, h * 2048:(h + 1) * 2048]
            scol = mtot[:, 0:1]
            n = 2048
            dname = "w1q_dram"
        else:
            for kk in range(2):
                jb = (i - 16) * 2 + kk
                src = D["w2T"][jb * P:(jb + 1) * P, :]
                dst = w2q_d[jb * P:(jb + 1) * P, :]
                dma(SY, wstage[:, 0:DM], src, [], ["wstage"])
                op(V, E_ts(g1t[:, 0:DM], wstage[:, 0:DM], mtot[:, 1:2], 0.5,
                           ALU.mult, ALU.is_ge), ["wstage", "mtot"], ["g1t"])
                op(V, E_ts(g2t[:, 0:DM], wstage[:, 0:DM], mtot[:, 1:2], -0.5,
                           ALU.mult, ALU.is_le), ["wstage", "mtot"], ["g2t"])
                op(V, E_tt(g1t[:, 0:DM], g1t[:, 0:DM], g2t[:, 0:DM],
                           ALU.subtract), ["g1t", "g2t"], ["g1t"])
                dma(SY, dst, g1t[:, 0:DM], ["g1t"], ["w2q_dram"])
            return
        dma(SY, wstage[:], src, [], ["wstage"])
        for hh2 in range(2):
            slh = slice(hh2 * DM, (hh2 + 1) * DM)
            op(V, E_stt(g1t[:], wstage[:, slh], scol,
                        bap(CT["cmisc"], [(0, DM)], offset=4), ALU.mult, ALU.is_ge),
               ["wstage", "mtot", "k_cmisc"], ["g1t"])
            op(V, E_stt(g2t[:], wstage[:, slh], scol,
                        bap(CT["cmisc"], [(0, DM)], offset=3), ALU.mult, ALU.is_le),
               ["wstage", "mtot", "k_cmisc"], ["g2t"])
            op(V, E_tt(g1t[:], g1t[:], g2t[:], ALU.subtract),
               ["g1t", "g2t"], ["g1t"])
            dma(SY, dst.tensor_handle_unused if False else dst_half(dst, hh2),
                g1t[:], ["g1t"], [dname])

    def m_finalize():
        op(G, E_memset(ftmp[:], 0.0), ["wstage"], ["ftmp"])
        op(V, E_red(ftmp[:, 0:1], msum[:, 0:32], ALU.add), ["msum"], ["ftmp"])
        op(V, E_red(ftmp[:, 1:2], msum[:, 32:64], ALU.add), ["msum"], ["ftmp"])
        op(T, E_tr(PG[:, 128:256], ftmp[:], CT["ident32"][:]),
           ["ftmp", "k_ident32", "PG0"], ["PG0"])
        op(S, E_act(mrowT[0:2, slice(0, P)], PG[0:2, 128:256], AF.Copy),
           ["PG0", "wstage"], ["mrowT"])
        op(V, E_red(sq1[0:2, 0:1], mrowT[0:2, slice(0, P)], ALU.add), ["mrowT"], ["sq1"])
        NTOT = float(DM * DH)
        op(V, E_recip(sq1[:, 1:2], sq1[:, 0:1]), ["sq1"], ["sq1"], selfdep=True)
        op(V, E_ts(sq1[0:2, 2:3], sq1[0:2, 1:2], NTOT, None, ALU.mult), ["sq1"], ["sq1"], selfdep=True)
        op(V, E_ts(sq1[0:2, 3:4], sq1[0:2, 0:1], 1.0 / NTOT, None, ALU.mult),
           ["sq1"], ["sq1"], selfdep=True)
        op(V, E_stt(ftmp[0:2, slice(0, P)], bap(sq1, [(0, P)], offset=2, pcount=2),
                    1.0, zap(P, pc=2), ALU.mult, ALU.add),
           ["sq1", "wstage", "zcol"], ["ftmp"], selfdep=True)
        op(T, E_tr(PG[:, 128:256], ftmp[0:P, slice(0, P)], CT["ident32"][:]),
           ["ftmp", "k_ident32", "PG0"], ["PG0"])
        op(S, E_act(mtot[:, 0:2], PG[:, 128:130], AF.Copy), ["PG0"], ["mtot"])
        op(V, E_stt(ftmp[0:2, slice(0, P)], bap(sq1, [(0, P)], offset=3, pcount=2),
                    1.0, zap(P, pc=2), ALU.mult, ALU.add),
           ["sq1", "wstage", "zcol"], ["ftmp"], selfdep=True)
        op(T, E_tr(PG[:, 128:256], ftmp[0:P, slice(0, P)], CT["ident32"][:]),
           ["ftmp", "k_ident32", "PG0"], ["PG0"])
        op(S, E_act(mtot[:, 2:4], PG[:, 128:130], AF.Copy), ["PG0"], ["mtot"])

    def mamba_phase(d):
        fwd = d == "f"
        ident = CT["ident16"][:]
        st_order = list(range(NST)) if fwd else list(range(NST - 1, -1, -1))
        halo_off = 0 if fwd else Q
        tok_off = 3 if fwd else 0
        cp_src = Q if fwd else 0
        cp_dst = 0 if fwd else Q
        dstrict = "tri_strict_" + ("b" if d == "f" else "f")

        op(G, E_memset(h0[:], 0.0), [], ["h0"])
        op(G, E_cp(h0b[:], h0[:]), ["h0"], ["h0b"])
        op(G, E_memset(bap(xbcpre, [(Q + 3, 18), (1, 3)], offset=halo_off), 0.0),
           [], ["xbcpre"])

        def part1_nz(st, par):
            dtr, dtrn = (dtraw, "dtraw") if par == 0 else (dtrawB, "dtrawB")
            rows = slice(st * Q, (st + 1) * Q)
            dma(G, u16[:], D["u"][rows, :], [], ["u16"])
            for j in range(8):
                op(T, E_trg(PX[:, j * 128:(j + 1) * 128],
                            u16[:, j * 128:(j + 1) * 128], ident, j == 0, j == 7),
                   ["u16", "k_ident16", "PX0"], ["PX0"])
            op(S, E_act(uT[:], PX[:, 0:1024], AF.Copy), ["PX0"], ["uT"])
            for h in range(2):
                for g in range(2):
                    for kd in range(8):
                        c0 = DI + h * 1024 + g * 512
                        op(T, E_mm(PB[:, g * 512:(g + 1) * 512],
                                   uT[:, kd * Q:(kd + 1) * Q],
                                   wbig[:, kd * DIP + c0:kd * DIP + c0 + 512],
                                   kd == 0, kd == 7),
                           ["uT", "wbig", f"PB{g}"], [f"PB{g}"])
                op(S, E_act(xbctm[:, h * 1024:(h + 1) * 1024], PB[:, 0:1024],
                            AF.Copy), ["PB0", "PB1"], ["xbctm"])
            for kd in range(8):
                op(T, E_mm(PA[:, 0:256], uT[:, kd * Q:(kd + 1) * Q],
                           wbig[:, kd * DIP + 2 * DI:kd * DIP + 2 * DI + 256],
                           kd == 0, kd == 7), ["uT", "wbig", "PA0"], ["PA0"])
            op(S, E_act(xbctm[:, DI:CD], PA[:, 0:256], AF.Copy), ["PA0"], ["xbctm"])
            for kd in range(8):
                op(T, E_mm(PG[:, 0:32], uT[:, kd * Q:(kd + 1) * Q],
                           wbig[:, kd * DIP + DI + CD:kd * DIP + DI + CD + 32],
                           kd == 0, kd == 7), ["uT", "wbig", "PG0"], ["PG0"])
            op(S, E_act(dtr[:], PG[:, 0:32], AF.Copy), ["PG0"], [dtrn])

        def part1_z(st):
            for h in range(2):
                for g in range(2):
                    for kd in range(8):
                        c0 = h * 1024 + g * 512
                        op(T, E_mm(PB[:, g * 512:(g + 1) * 512],
                                   uT[:, kd * Q:(kd + 1) * Q],
                                   wbig[:, kd * DIP + c0:kd * DIP + c0 + 512],
                                   kd == 0, kd == 7),
                           ["uT", "wbig", f"PB{g}"], [f"PB{g}"])
                op(S, E_act(zsil[:, h * 1024:(h + 1) * 1024], PB[:, 0:1024],
                            AF.Silu), ["PB0", "PB1"], ["zsil"])

        def convsec(st):
            for grp, cnt in ((0, 8), (1, 8), (2, 2)):
                bank = grp % 2
                pxb = f"PX{bank}"
                for j in range(cnt):
                    jc = grp * 8 + j
                    op(T, E_trg(PX[:, bank * 1024 + j * 128:bank * 1024 + (j + 1) * 128],
                                xbctm[:, jc * 128:(jc + 1) * 128], ident,
                                j == 0, j == cnt - 1),
                       ["xbctm", "k_ident16", pxb], [pxb])
                out_ap = bap(xbcpre, [(Q + 3, cnt), (1, Q)],
                             offset=(grp * 8) * (Q + 3) + tok_off)
                op(S, E_act(out_ap, PX[:, bank * 1024:bank * 1024 + cnt * 128],
                            AF.Copy), [pxb], ["xbcpre"])
            gtmp = wg1[:, 0:Q]
            for jc in range(18):
                nm = f"xbcc{jc}"
                dst = xbcc[:, jc * Q:(jc + 1) * Q]
                for k in range(4):
                    widx = k if fwd else 3 - k
                    src2 = xbcpre[:, jc * (Q + 3) + k:jc * (Q + 3) + k + Q]
                    wcol = cwS[d][:, jc * 4 + widx:jc * 4 + widx + 1]
                    if k == 0:
                        fn = E_stt(dst, src2, wcol,
                                   bap(cbS[d], [(0, Q)], offset=jc),
                                   ALU.mult, ALU.add)
                        op(V, fn, ["xbcpre", f"cw{d}", f"cb{d}", nm], [nm])
                    elif jc < 9:
                        fn = E_stt(dst, src2, wcol, dst, ALU.mult, ALU.add)
                        op(V, fn, ["xbcpre", f"cw{d}", f"cb{d}", nm], [nm])
                    else:
                        wb = bap(cwS[d], [(0, Q)], offset=jc * 4 + widx)
                        op(G, E_tt(gtmp, src2, wb, ALU.mult),
                           ["xbcpre", f"cw{d}", "wg1"], ["wg1"])
                        op(G, E_tt(dst, gtmp, dst, ALU.add),
                           ["wg1", nm], [nm])
            allx = [f"xbcc{j}" for j in range(18)]
            op(S, E_act(xbcc[:], xbcc[:], AF.Silu), allx, allx)
            op(V, E_cp(bap(xbcpre, [(Q + 3, 18), (1, 3)], offset=cp_dst),
                       bap(xbcpre, [(Q + 3, 18), (1, 3)], offset=cp_src)),
               ["xbcpre"], ["xbcpre"])

        def rest_a(st, sti):
            dtr, dtrn = (dtraw, "dtraw") if sti % 2 == 0 else (dtrawB, "dtrawB")
            op(V, E_tt(sm1[:], dtr[:], dtbS[d][:], ALU.add),
               [dtrn, f"dtb{d}"], ["sm1"])
            op(S, E_act(sm2[:], sm1[:], AF.Abs), ["sm1"], ["sm2"])
            op(S, E_act(sm3[:], sm2[:], AF.Exp, scale=CT["cmisc"][:, 2:3]),
               ["sm2", "k_cmisc"], ["sm3"])
            op(S, E_act(sm2[:], sm3[:], AF.Ln, bias=CT["cmisc"][:, 0:1]),
               ["sm3", "k_cmisc"], ["sm2"])
            op(V, E_stt(dtf[:], sm1[:], 0.0, sm2[:], ALU.max, ALU.add),
               ["sm1", "sm2"], ["dtf"])
            op(V, E_tt(p16[:], dtf[:], aexp[d][:], ALU.mult), ["dtf", f"al{d}"], ["p16"])
            op(V, E_stt(a16t[:], dtf[:], -1.0, aexp[d][:], ALU.mult, ALU.mult),
               ["dtf", f"al{d}"], ["a16t"])
            op(T, E_mm(PA[:, 0:32], CT[f"tri_incl_{d}"][:], a16t[:], True, False),
               [f"k_tri_incl_{d}", "a16t", "PA0"], ["PA0"])
            op(T, E_mm(PA[:, 32:64], CT[f"tri_strict_{d}"][:], a16t[:], False, False),
               [f"k_tri_strict_{d}", "a16t"], ["PA0"])
            op(T, E_mm(PA[:, 64:96], CT["ones_bf"][:], a16t[:], False, True),
               ["k_ones_bf", "a16t"], ["PA0"])
            op(S, E_act(expac[:], PA[:, 0:32], AF.Exp), ["PA0"], ["expac"])
            op(S, E_act(sm3[:], PA[:, 32:64], AF.Exp), ["PA0"], ["sm3"])
            op(S, E_act(cdb[:], PA[:, 64:96], AF.Exp), ["PA0"], ["cdb"])
            op(V, E_tt(dtrv[:], dtf[:], sm3[:], ALU.mult), ["dtf", "sm3"], ["dtrv"])
            op(T, E_mm(PG[:, 128:256], xbcc[:, 16 * Q:17 * Q], xbcc[:, 17 * Q:18 * Q],
                       True, True), ["xbcc16", "xbcc17", "PG0"], ["PG0"])
            px = PX[:, 0:128]
            op(T, E_tr(px, xbcc[:, 16 * Q:17 * Q], ident),
               ["xbcc16", "k_ident16", "PX0"], ["PX0"])
            op(S, E_act(btok[:], px, AF.Copy), ["PX0"], ["btok"])
            for grp in range(2):
                bank = grp % 2
                pxb = f"PX{bank}"
                for j in range(8):
                    jc = grp * 8 + j
                    nm = f"xbcc{jc}"
                    op(T, E_trg(PX[:, bank * 1024 + j * 128:bank * 1024 + (j + 1) * 128],
                                xbcc[:, jc * Q:(jc + 1) * Q], ident, j == 0, j == 7),
                       [nm, "k_ident16", pxb], [pxb])
                op(S, E_act(xTs[:, grp * 1024:(grp + 1) * 1024],
                            PX[:, bank * 1024:bank * 1024 + 1024], AF.Copy),
                   [pxb], ["xTs"])
            dt_ap = bap(dtf, [(1, NH), (0, HD)])
            dtrv_ap = bap(dtrv, [(1, NH), (0, HD)])
            dv_ap = bap(dvecS[d], [(1, NH), (0, HD)])
            op(V, E_tt(r3(xprime[:], NH), r3(xTs[:], NH), dt_ap, ALU.mult),
               ["xTs", "dtf"], ["xprime"])
            op(V, E_tt(r3(xpp[:], NH), r3(xTs[:], NH), dtrv_ap, ALU.mult),
               ["xTs", "dtrv"], ["xpp"])
            op(V, E_tt(r3(xdt[:], NH), r3(xTs[:], NH), dv_ap, ALU.mult),
               ["xTs", f"dv{d}"], ["xdt"])
            for H, wgt, wgn in ((0, wg1, "wg1"), (1, wg2, "wg2")):
                hoff = H * 16
                op(V, E_tt(r3(rhsa[:], 16), bap(a16t, [(1, 16), (0, Q)], offset=hoff),
                           bap(CT[f"tri_incl_{d}"], [(0, 16), (1, Q)]), ALU.mult),
                   ["a16t", f"k_tri_incl_{d}"], ["rhsa"])
                for q in range(2):
                    qoff = q * 1024
                    for g in range(2):
                        sl = PB[:, 1024 + g * 512:1024 + (g + 1) * 512]
                        pbn = f"PB{2 + g}"
                        op(T, E_mm(sl, CT["ones_bf"][:],
                                   rhsa[:, qoff + g * 512:qoff + (g + 1) * 512],
                                   True, False), ["k_ones_bf", "rhsa", pbn], [pbn])
                        op(T, E_mm(sl, CT[f"tri_incl_{d}"][:],
                                   bap(p16, [(1, 4), (0, Q)],
                                       offset=hoff + q * 8 + g * 4),
                                   False, False), [f"k_tri_incl_{d}", "p16"], [pbn])
                        op(T, E_mm(sl, CT[dstrict][:],
                                   bap(CT["idneg"], [(0, 4), (1, Q)]),
                                   False, True), ["k_" + dstrict, "k_idneg"], [pbn])
                    op(S, E_act(wgt[:, qoff:qoff + 1024], PB[:, 1024:2048], AF.Exp),
                       ["PB2", "PB3"], [wgn])
                gr_ap = AP(PG[:].tensor, 128, [[512, P], [0, 16], [1, Q]])
                op(V, E_tt(r3(wgt[:], 16), r3(wgt[:], 16), gr_ap, ALU.mult),
                   [wgn, "PG0"], [wgn])
            exa = bap(expac, [(1, NH), (0, HD)])
            for Hh in range(2):
                off = Hh * 1024
                for g in range(2):
                    pbn = f"PB{2 + g}"
                    op(T, E_mm(PB[:, 1024 + g * 512:1024 + (g + 1) * 512],
                               xbcc[:, 17 * Q:18 * Q],
                               h0b[:, off + g * 512:off + (g + 1) * 512], True, True),
                       ["xbcc", "h0b", pbn], [pbn])
                exa_h = bap(expac, [(1, 16), (0, HD)], offset=Hh * 16)
                op(V, E_tt(r3(y0t[:, off:off + 1024], 16),
                           r3(PB[:, 1024:2048], 16), exa_h, ALU.mult),
                   ["PB2", "PB3", "expac"], ["y0t"])
                for g in range(2):
                    pbn = f"PB{2 + g}"
                    op(T, E_mm(PB[:, 1024 + g * 512:1024 + (g + 1) * 512],
                               CT["ident16"][:],
                               xdt[:, off + g * 512:off + (g + 1) * 512], True, False),
                       ["k_ident16", "xdt", pbn], [pbn])
                    for hh in range(8):
                        h = Hh * 16 + g * 8 + hh
                        wgt, wgn = (wg1, "wg1") if h < 16 else (wg2, "wg2")
                        op(T, E_mm(PB[:, 1024 + (g * 8 + hh) * HD:
                                       1024 + (g * 8 + hh + 1) * HD],
                                   wgt[:, (h % 16) * Q:((h % 16) + 1) * Q],
                                   xprime[:, h * HD:(h + 1) * HD], False, hh == 7),
                           [wgn, "xprime", pbn], [pbn])
                op(V, E_stt(t1t[:, off:off + 1024], PB[:, 1024:2048], 1.0,
                            y0t[:, off:off + 1024], ALU.mult, ALU.add),
                   ["PB2", "PB3", "y0t"], ["t1t"])
            op(V, E_tt(t1t[:], t1t[:], zsil[:], ALU.mult), ["t1t", "zsil"], ["t1t"])

        def rest_b(st, sti):
            rows = slice(st * Q, (st + 1) * Q)
            op(S, E_act(rhsa[:], t1t[:], AF.Square, accum=stat[:, 0:1]),
               ["t1t", "rhsa"], ["rhsa", "stat"])
            op(V, E_ts(stat[:, 1:2], stat[:, 0:1], 1.0 / DI, 1e-5, ALU.mult, ALU.add),
               ["stat"], ["stat"], selfdep=True)
            op(S, E_act(stat[:, 2:3], stat[:, 1:2], AF.Ln), ["stat"], ["stat"])
            op(S, E_act(stat[:, 3:4], stat[:, 2:3], AF.Exp, scale=CT["cmisc"][:, 3:4]),
               ["stat", "k_cmisc"], ["stat"])
            op(S, E_act(ynt[:], t1t[:], AF.Copy, scale=stat[:, 3:4]),
               ["t1t", "stat"], ["ynt"], selfdep=True)
            for grp in range(2):
                bank = grp % 2
                pxb = f"PX{bank}"
                for j in range(8):
                    jc = grp * 8 + j
                    op(T, E_trg(PX[:, bank * 1024 + j * 128:bank * 1024 + (j + 1) * 128],
                                ynt[:, jc * 128:(jc + 1) * 128], ident, j == 0, j == 7),
                       ["ynt", "k_ident16", pxb], [pxb])
                op(S, E_act(yTt[:, grp * 1024:(grp + 1) * 1024],
                            PX[:, bank * 1024:bank * 1024 + 1024], AF.Copy),
                   [pxb], ["yTt"])
            for Hh in range(2):
                off = Hh * 1024
                for g in range(2):
                    pbn = f"PB{2 + g}"
                    op(T, E_mm(PB[:, 1024 + g * 512:1024 + (g + 1) * 512], btok[:],
                               xpp[:, off + g * 512:off + (g + 1) * 512], True, True),
                       ["btok", "xpp", pbn], [pbn])
                for h in range(16):
                    hh = Hh * 16 + h
                    op(V, E_stt(h0[:, hh * HD:(hh + 1) * HD],
                                h0[:, hh * HD:(hh + 1) * HD],
                                cdb[:, hh:hh + 1], PB[:, 1024 + h * HD:1024 + (h + 1) * HD],
                                ALU.mult, ALU.add),
                       ["h0", "cdb", f"PB{2 + (h * HD * 4) // 2048}"], ["h0"])
            op(G, E_cp(h0b[:], h0[:]), ["h0"], ["h0b"])
            for g in range(2):
                pbn = f"PB{2 + g}"
                for kc in range(16):
                    op(T, E_mm(PB[:, 1024 + g * 512:1024 + (g + 1) * 512],
                               yTt[:, kc * Q:(kc + 1) * Q],
                               wsec[:, kc * DM + g * 512:kc * DM + (g + 1) * 512],
                               kc == 0, kc == 15),
                       ["yTt", "wsec", pbn], [pbn])
            pb23 = ["PB2", "PB3"]
            if d == "f":
                op(V, E_cp(comb[:], PB[:, 1024:2048]), pb23, ["comb"])
                dma(SY, outf_d[rows, :], comb[:], ["comb"], ["outf_dram"])
            else:
                dma(SY, comb[:], outf_d[rows, :], ["outf_dram"], ["comb"])
                dma(SY, uf32[:], D["u"][rows, :], [], ["uf32"])
                op(V, E_tt(comb[:], PB[:, 1024:2048], comb[:], ALU.add),
                   pb23 + ["comb"], ["comb"])
                op(V, E_tt(comb[:], comb[:], uf32[:], ALU.add),
                   ["comb", "uf32"], ["comb"])
                op(V, E_red(stat[:, 0:1], comb[:], ALU.add), ["comb"], ["stat"])
                op(V, E_ts(stat[:, 1:2], stat[:, 0:1], 1.0 / DM, None, ALU.mult),
                   ["stat"], ["stat"], selfdep=True)
                op(S, E_act(sgq[:], comb[:], AF.Square, accum=stat[:, 2:3]),
                   ["comb", "sgq"], ["sgq", "stat"])
                op(V, E_tt(stat[:, 3:4], stat[:, 1:2], stat[:, 1:2], ALU.mult),
                   ["stat"], ["stat"], selfdep=True)
                op(V, E_stt(stat[:, 4:5], stat[:, 2:3], 1.0 / DM, stat[:, 3:4],
                            ALU.mult, ALU.subtract), ["stat"], ["stat"], selfdep=True)
                op(S, E_act(stat[:, 5:6], stat[:, 4:5], AF.Ln, bias=CT["cmisc"][:, 1:2]),
                   ["stat", "k_cmisc"], ["stat"])
                op(S, E_act(stat[:, 6:7], stat[:, 5:6], AF.Exp, scale=CT["cmisc"][:, 3:4]),
                   ["stat", "k_cmisc"], ["stat"])
                op(V, E_stt(stat[:, 7:8], stat[:, 1:2], -1.0, stat[:, 6:7],
                            ALU.mult, ALU.mult), ["stat"], ["stat"], selfdep=True)
                op(S, E_act(comb[:], comb[:], AF.Identity, scale=stat[:, 6:7],
                            bias=stat[:, 7:8]), ["comb", "stat"], ["comb"],
                   selfdep=True)
                dma(SY, x1_d[rows, :], comb[:], ["comb"], ["x1_dram"])

        part1_nz(st_order[0], 0)
        part1_z(st_order[0])
        convsec(st_order[0])
        for sti, st in enumerate(st_order):
            if d == "f":
                wprep_abs(sti)
            else:
                wprep_tern(sti)
            if sti + 1 < NST:
                part1_nz(st_order[sti + 1], (sti + 1) % 2)
            rest_a(st, sti)
            if sti + 1 < NST:
                part1_z(st_order[sti + 1])
            rest_b(st, sti)
            if sti + 1 < NST:
                convsec(st_order[sti + 1])

    def ffn_phase():
        ident = CT["ident16"][:]
        dma(SY, r3(wbig[:, 0:8 * DH], 8), w1q_d.rearrange("(j p) c -> p j c", p=P),
            ["w1q_dram"], ["wbig"])
        dma(SY, AP(wsec[:].tensor, 0, [[32 * DM, P], [DM, 32], [1, DM]]),
            w2q_d.rearrange("(j p) c -> p j c", p=P), ["w2q_dram"],
            ["wsec", "wg1", "wg2", "rhsa", "yTt", "xbcpre", "xbctm", "btok"]
            + [f"xbcc{j}" for j in range(18)])
        op(G, E_memset(h1t[:], 0.0),
           [], ["h1t", "zsil", "xprime", "xpp", "xdt"])
        op(G, E_memset(h2t[:, 0:4], 0.0),
           [], ["h2t", "xTs", "y0t", "t1t", "ynt"])
        op(G, E_memset(uu[:, 0:4], 0.0),
           [], ["u16", "uT", "macc", "mincc", "sgnc", "sgq"])
        op(G, E_memset(h0[:, 0:4], 0.0), [], ["h0", "x1t"])
        op(G, E_memset(uf32[:, 0:4], 0.0), [], ["uf32", "qstage"])
        op(G, E_memset(g1t[:, 0:4], 0.0), [], ["g1t"])
        op(G, E_memset(h0b[:, 0:4], 0.0), [], ["h0b", "q1b"])
        op(G, E_memset(g2t[:, 0:4], 0.0), [], ["g2t", "q1hb"])
        op(G, E_memset(wstage[:, 0:4], 0.0), [], ["wstage", "tqc", "s2B", "s2rep", "x1h"])
        op(G, E_memset(q1hb[:], 0.0), ["q1hb"], ["q1hb"])

        for st in range(NFST):
            dma(SY, x1t[:, 0:DM], x1_d[st * FST:st * FST + P, :],
                ["x1_dram"], ["x1t"])
            dma(SY, x1t[:, DM:2 * DM], x1_d[st * FST + P:st * FST + 2 * P, :],
                ["x1_dram"], ["x1t"])
            has_halo = st < NFST - 1
            if has_halo:
                dma(SY, wstage[0:1, 256:1280], x1_d[(st + 1) * FST:(st + 1) * FST + 1, :],
                    ["x1_dram"], ["x1h"])
            for b in range(2):
                xb = x1t[:, b * DM:(b + 1) * DM]
                op(V, E_red(sq1[:, 0:1], xb, ALU.max, absv=True), ["x1t"], ["sq1"])
                op(V, E_ts(sq1[:, 1:2], sq1[:, 0:1], 1e-5, None, ALU.max),
                   ["sq1"], ["sq1"], selfdep=True)
                op(V, E_recip(sq1[:, 2:3], sq1[:, 1:2]), ["sq1"], ["sq1"], selfdep=True)
                op(V, E_ts(sq1[:, 3:4], sq1[:, 2:3], 127.0, None, ALU.mult),
                   ["sq1"], ["sq1"], selfdep=True)
                op(V, E_ts(sq1[:, 4:5], sq1[:, 1:2], 1.0 / 127.0, None, ALU.mult),
                   ["sq1"], ["sq1"], selfdep=True)
                op(V, E_tt(sq1[:, 5:6], sq1[:, 4:5], mtot[:, 2:3], ALU.mult),
                   ["sq1", "mtot"], ["sq1"], selfdep=True)
                op(S, E_act(sgq[:], xb, AF.Sign), ["x1t"], ["sgq"])
                for hp in range(4):
                    sl = slice(hp * 256, (hp + 1) * 256)
                    op(S, E_act(qstage[:, slice(0, 256)], xb[:, sl], AF.Copy,
                                scale=sq1[:, 3:4]), ["x1t", "sq1"], ["qstage"])
                    op(V, E_stt(iqc[:], sgq[:, sl], beta[:], qstage[:, slice(0, 256)],
                                ALU.mult, ALU.add), ["sgq", "beta", "qstage"], ["iqc"])
                    op(V, E_stt(q1b[:, b * DM + hp * 256:b * DM + (hp + 1) * 256],
                                iqc[:], sq1[:, 5:6], zap(256), ALU.mult, ALU.add),
                       ["iqc", "zcol", "sq1"], ["q1b"])
            if has_halo:
                op(V, E_red(sq1[0:1, 0:1], wstage[0:1, 256:1280], ALU.max, absv=True),
                   ["x1h"], ["sq1"])
                op(V, E_ts(sq1[0:1, 1:2], sq1[0:1, 0:1], 1e-5, None, ALU.max),
                   ["sq1"], ["sq1"], selfdep=True)
                op(V, E_recip(sq1[0:1, 2:3], sq1[0:1, 1:2]), ["sq1"], ["sq1"], selfdep=True)
                op(V, E_ts(sq1[0:1, 3:4], sq1[0:1, 2:3], 127.0, None, ALU.mult),
                   ["sq1"], ["sq1"], selfdep=True)
                op(V, E_ts(sq1[0:1, 4:5], sq1[0:1, 1:2], 1.0 / 127.0, None, ALU.mult),
                   ["sq1"], ["sq1"], selfdep=True)
                op(V, E_tt(sq1[0:1, 5:6], sq1[0:1, 4:5], mtot[0:1, 2:3], ALU.mult),
                   ["sq1", "mtot"], ["sq1"], selfdep=True)
                op(S, E_act(sgq[0:1, slice(0, DM)], wstage[0:1, 256:1280], AF.Sign), ["x1h"], ["sgq"])
                for hp in range(4):
                    sl = slice(hp * 256, (hp + 1) * 256)
                    op(S, E_act(qstage[0:1, slice(0, 256)],
                                wstage[0:1, 256 + hp * 256:256 + (hp + 1) * 256],
                                AF.Copy, scale=sq1[0:1, 3:4]), ["x1h", "sq1"], ["qstage"])
                    op(V, E_stt(iqc[0:1, slice(0, 256)], sgq[0:1, sl], beta[0:1, 0:1],
                                qstage[0:1, slice(0, 256)], ALU.mult, ALU.add),
                       ["sgq", "beta", "qstage"], ["iqc"])
                    op(V, E_stt(q1hb[0:1, sl], iqc[0:1, slice(0, 256)], sq1[0:1, 5:6],
                                zap(256, pc=1), ALU.mult, ALU.add),
                       ["iqc", "zcol", "sq1"], ["q1hb"])
            else:
                op(G, E_memset(q1hb[0:1, :], 0.0), [], ["q1hb"])
            for grp in range(2):
                bank = grp
                pxb = f"PX{bank}"
                for j in range(8):
                    kd, b = (grp * 4 + j // 2), j % 2
                    op(T, E_trg(PX[:, bank * 1024 + j * 128:bank * 1024 + (j + 1) * 128],
                                q1b[:, b * DM + kd * 128:b * DM + (kd + 1) * 128],
                                ident, j == 0, j == 7),
                       ["q1b", "k_ident16", pxb], [pxb])
                out_ap = AP(q1T[:].tensor, q1T[:].offset + (grp * 4) * (FST + 1),
                            [[q1T[:].ap[0][0], P], [FST + 1, 4], [P, 2], [1, P]])
                op(S, E_act(out_ap, PX[:, bank * 1024:bank * 1024 + 1024], AF.Copy),
                   [pxb], ["q1T", "wbig"])
            for grp in range(2):
                bank = grp
                pxb = f"PX{bank}"
                for j in range(4):
                    kd = grp * 4 + j
                    op(T, E_trg(PX[:, bank * 1024 + j * 128:bank * 1024 + (j + 1) * 128],
                                q1hb[:, kd * 128:(kd + 1) * 128], ident,
                                j == 0, j == 3),
                       ["q1hb", "k_ident16", pxb], [pxb])
                in_ap = AP(PX[:].tensor, bank * 1024, [[2048, P], [128, 4], [1, 1]])
                out_ap = AP(q1T[:].tensor, q1T[:].offset + (grp * 4) * (FST + 1) + FST,
                            [[q1T[:].ap[0][0], P], [FST + 1, 4], [1, 1]])
                op(S, E_act(out_ap, in_ap, AF.Copy), [pxb], ["q1T", "wbig"])
            if st == _DBG.get("fst", -1):
                dbg_point("q1", {
                    "q1b": (q1b, "q1b", BF16, DI, P),
                    "q1T": (q1T, "q1T", BF16, 8 * (FST + 1), P),
                    "x1t": (x1t, "x1t", F32, DI, P),
                    "sgq": (sgq, "sgq", BF16, DM, P),
                    "sq1": (sq1, "sq1", F32, 4, P),
                    "beta": (beta, "beta", F32, 1, P),
                    "iqc": (iqc, "iqc", I32, 256, P),
                    "qstage": (qstage, "qstage", F32, 512, P),
                    "i32s": (i32s, "i32s", I32, 1, P),
                    "cmisc": (CT["cmisc"], "k_cmisc", F32, 6, P)})
            op(V, E_cp(bap(h1t, [(FST + 2, 32), (1, 1)], offset=0),
                       bap(h1t, [(FST + 2, 32), (1, 1)], offset=FST)),
               ["h1t"], ["h1t"])
            for jc in range(32):
                ptile, pn = (PA, "PA0") if jc % 2 == 0 else (PG, "PG0")
                for kd in range(8):
                    op(T, E_mm(ptile[:, 0:FST + 1],
                               wbig[:, kd * DH + jc * 128:kd * DH + (jc + 1) * 128],
                               q1T[:, kd * (FST + 1):(kd + 1) * (FST + 1)],
                               kd == 0, kd == 7),
                       ["wbig", "q1T", pn], [pn])
                op(S, E_act(h1t[:, jc * (FST + 2) + 1:jc * (FST + 2) + FST + 2],
                            ptile[:, 0:FST + 1], AF.Copy), [pn], ["h1t"])
            if st == _DBG.get("fst", -1):
                dbg_point("h1", {"h1t": (h1t, "h1t", BF16, 32 * (FST + 2), P),
                                 "wq0": (wbig.view(0, 4096), "wbig", BF16, 4096, P),
                                 "q1Tb": (q1T, "q1T", BF16, 8 * (FST + 1), P)})
            for jc in range(32):
                dst = h2t[:, jc * FST:(jc + 1) * FST]
                for k in range(3):
                    src = h1t[:, jc * (FST + 2) + k:jc * (FST + 2) + k + FST]
                    if k == 0:
                        fn = E_stt(dst, src, fcwS[:, jc * 3:jc * 3 + 1],
                                   bap(fcbS, [(0, FST)], offset=jc),
                                   ALU.mult, ALU.add)
                    else:
                        fn = E_stt(dst, src, fcwS[:, jc * 3 + k:jc * 3 + k + 1],
                                   dst, ALU.mult, ALU.add)
                    op(V, fn, ["h1t", "fcw", "fcb", "h2t"], ["h2t"])
            op(S, E_act(h2t[:], h2t[:], AF.Silu), ["h2t"], ["h2t"])
            op(V, E_stt(macc[:], h2t[:, 0:FST], 1.0, h2t[:, 0:FST],
                        ALU.mult, ALU.max), ["h2t"], ["macc"])
            op(V, E_stt(mincc[:], h2t[:, 0:FST], 1.0, h2t[:, 0:FST],
                        ALU.mult, ALU.min), ["h2t"], ["mincc"])
            for jc in range(1, 32):
                sl = h2t[:, jc * FST:(jc + 1) * FST]
                op(V, E_tt(macc[:], macc[:], sl, ALU.max), ["h2t", "macc"], ["macc"])
                op(V, E_tt(mincc[:], mincc[:], sl, ALU.min), ["h2t", "mincc"], ["mincc"])
            op(V, E_stt(macc[:], mincc[:], -1.0, macc[:], ALU.mult, ALU.max),
               ["macc", "mincc"], ["macc"])
            for b in range(2):
                slot = (b % 2) * 1024
                px = PX[:, slot:slot + 128]
                pxb = f"PX{b % 2}"
                op(T, E_tr(px, macc[:, b * 128:(b + 1) * 128], ident),
                   ["macc", "k_ident16", pxb], [pxb])
                op(V, E_red(s2c[:, b:b + 1], px, ALU.max), [pxb], ["s2c"])
            op(V, E_stt(s2cl[:], s2c[:], 1.0, bap(CT["cmisc"], [(0, 2)], offset=1),
                        ALU.mult, ALU.max), ["s2c", "k_cmisc"], ["s2cl"], selfdep=True)
            for bb in range(2):
                op(V, E_recip(f2c[:, bb:bb + 1], s2cl[:, bb:bb + 1]), ["s2cl"], ["f2c"], selfdep=True)
            op(V, E_stt(s2c[:], f2c[:], 127.0, zap(2), ALU.mult, ALU.add),
               ["f2c", "zcol"], ["s2c"], selfdep=True)
            op(V, E_stt(f2c[:], s2cl[:], 1.0 / 127.0, zap(2), ALU.mult, ALU.add),
               ["s2cl", "zcol"], ["f2c"], selfdep=True)
            op(V, E_tt(f2c[:], f2c[:], bap(mtot, [(0, 2)], offset=3), ALU.mult),
               ["f2c", "mtot"], ["f2c"], selfdep=True)
            for b in range(2):
                op(V, E_stt(s2rep[:], bap(s2c, [(0, P)], offset=b), 1.0, zap(P),
                            ALU.mult, ALU.add), ["s2c", "zcol"], ["s2rep"], selfdep=True)
                op(T, E_tr(PG[:, 128:256], s2rep[:], CT["ident32"][:]),
                   ["s2rep", "k_ident32", "PG0"], ["PG0"])
                op(S, E_act(s2B[:, b * 128:(b + 1) * 128], PG[:, 128:256], AF.Copy),
                   ["PG0"], ["s2B"])
            for jc in range(32):
                sl = h2t[:, jc * FST:(jc + 1) * FST]
                op(V, E_tt(tqc[:], sl, s2B[:], ALU.mult), ["h2t", "s2B"], ["tqc"])
                op(S, E_act(sgnc[:], sl, AF.Sign), ["h2t"], ["sgnc"])
                op(V, E_stt(iqc[:], sgnc[:], beta[:], tqc[:],
                            ALU.mult, ALU.add), ["sgnc", "beta", "tqc"], ["iqc"])
                op(V, E_stt(sl, iqc[:], 1.0, zap(FST), ALU.mult, ALU.add),
                   ["iqc", "zcol"], ["h2t"])
            if st == _DBG.get("fst", -1):
                dbg_point("h2", {"h2t": (h2t, "h2t", BF16, 32 * FST, P),
                                  "s2B": (s2B, "s2B", F32, FST, P),
                                  "f2c": (f2c, "f2c", F32, 2, P),
                                  "mtot": (mtot, "mtot", F32, 4, P)})
            for b in range(2):
                for half in range(2):
                    for jc in range(32):
                        op(T, E_mm(PB[:, half * 512:(half + 1) * 512],
                                   h2t[:, jc * FST + b * P:jc * FST + (b + 1) * P],
                                   wsec[:, jc * DM + half * 512:jc * DM + (half + 1) * 512],
                                   jc == 0, jc == 31),
                           ["h2t", "wsec", PB_BANKS[half]], [PB_BANKS[half]])
                pb01 = ["PB0", "PB1"]
                op(V, E_stt(comb[:], PB[:, 0:1024], f2c[:, b:b + 1],
                            x1t[:, b * DM:(b + 1) * DM], ALU.mult, ALU.add),
                   pb01 + ["f2c", "x1t"], ["comb"])
                op(V, E_red(stat[:, 0:1], comb[:], ALU.add), ["comb"], ["stat"])
                op(V, E_ts(stat[:, 1:2], stat[:, 0:1], 1.0 / DM, None, ALU.mult),
                   ["stat"], ["stat"], selfdep=True)
                op(S, E_act(sgq[:], comb[:], AF.Square, accum=stat[:, 2:3]),
                   ["comb", "sgq"], ["sgq", "stat"])
                op(V, E_tt(stat[:, 3:4], stat[:, 1:2], stat[:, 1:2], ALU.mult),
                   ["stat"], ["stat"], selfdep=True)
                op(V, E_stt(stat[:, 4:5], stat[:, 2:3], 1.0 / DM, stat[:, 3:4],
                            ALU.mult, ALU.subtract), ["stat"], ["stat"], selfdep=True)
                op(S, E_act(stat[:, 5:6], stat[:, 4:5], AF.Ln, bias=CT["cmisc"][:, 1:2]),
                   ["stat", "k_cmisc"], ["stat"])
                op(S, E_act(stat[:, 6:7], stat[:, 5:6], AF.Exp, scale=CT["cmisc"][:, 3:4]),
                   ["stat", "k_cmisc"], ["stat"])
                op(V, E_stt(stat[:, 7:8], stat[:, 1:2], -1.0, stat[:, 6:7],
                            ALU.mult, ALU.mult), ["stat"], ["stat"], selfdep=True)
                op(S, E_act(comb[:], comb[:], AF.Identity, scale=stat[:, 6:7],
                            bias=stat[:, 7:8]), ["comb", "stat"], ["comb"],
                   selfdep=True)
                dma(SY, out_d[st * FST + b * P:st * FST + (b + 1) * P, :], comb[:],
                    ["comb"], [])

    try:
        load_wi("f")
        load_wo("f")
        mamba_phase("f")
        if _DBG.get("stop") == "endF":
            t = nc.dram_tensor("dbg_outf", [L, DM], F32, kind="ExternalOutput").ap()
            dma(SY, t, outf_d, ["outf_dram"], [])
            raise _StopRec()
        m_finalize()
        load_wi("b")
        load_wo("b")
        mamba_phase("b")
        if _DBG.get("stop") == "endB":
            t = nc.dram_tensor("dbg_x1", [L, DM], F32, kind="ExternalOutput").ap()
            dma(SY, t, x1_d, ["x1_dram"], [])
            t2 = nc.dram_tensor("dbg_mtot", [P, 4], F32, kind="ExternalOutput").ap()
            dma(SY, t2, mtot[:], ["mtot"], [])
            raise _StopRec()
        ffn_phase()
    except _StopRec:
        pass

    for o in pr.ops:
        o.deps = [dd for dd in o.deps
                  if dd.engine != o.engine or dd.is_dma or o.selfdep]
        for dd in o.deps:
            dd.needs_inc = True
    counters = {}
    for o in pr.ops:
        if o.is_dma:
            continue
        if o.needs_inc:
            semn = "c_" + o.engine
            counters[semn] = counters.get(semn, 0) + 1
            o.token = (semn, counters[semn])

    eng_ops = {e: [] for e in ("tensor", "vector", "scalar", "gpsimd", "sync")}
    for o in pr.ops:
        eng_ops[o.engine].append(o)
    watermark = {e: {} for e in eng_ops}

    def emit(engine, eng):
        wm = watermark[engine]
        for o in eng_ops[engine]:
            for dd in o.deps:
                if dd.token is None:
                    continue
                semn, val = dd.token
                if wm.get(semn, 0) >= val:
                    continue
                eng.wait_ge(sems[semn], val)
                wm[semn] = val
            inst = o.fn(eng)
            if o.needs_inc and not o.is_dma:
                inst.then_inc(sems[o.token[0]], 1)

    with nc.Block() as blk:
        @blk.tensor
        def _(e):
            emit("tensor", e)
        @blk.vector
        def _(e):
            emit("vector", e)
        @blk.scalar
        def _(e):
            emit("scalar", e)
        @blk.gpsimd
        def _(e):
            emit("gpsimd", e)
            for slot in dma_state["gpsimd"]:
                if slot["count"] > 0:
                    e.wait_ge(sems[slot["name"]], slot["count"])
        @blk.sync
        def _(e):
            emit("sync", e)
            for slot in dma_state["sync"]:
                if slot["count"] > 0:
                    e.wait_ge(sems[slot["name"]], slot["count"])

    es.close()
    return nc



_NC_CACHE = {}


def _layout_inputs(ins, b):
    HC = host_consts()
    m = {"u": np.ascontiguousarray(np.asarray(ins["L0_attn_u"])[b], dtype=np.float32)}
    for d in ("f", "b"):
        m[f"wiT_{d}"] = np.ascontiguousarray(np.asarray(ins[f"L0_attn_Wi_{d}"]).T)
        m[f"woT_{d}"] = np.ascontiguousarray(np.asarray(ins[f"L0_attn_Wo_{d}"]).T)
        cw = np.asarray(ins[f"L0_attn_conv_w_{d}"])[:, 0, :]
        m[f"convw_{d}"] = np.ascontiguousarray(
            cw.reshape(18, P, 4).transpose(1, 0, 2).reshape(P, 72))
        m[f"convb_{d}"] = np.ascontiguousarray(
            np.asarray(ins[f"L0_attn_conv_b_{d}"]).reshape(18, P).T)
        m[f"dtb_{d}"] = np.ascontiguousarray(
            np.broadcast_to(np.asarray(ins[f"L0_attn_dt_bias_{d}"]), (P, NH)))
        m[f"alog_{d}"] = np.ascontiguousarray(
            np.broadcast_to(np.asarray(ins[f"L0_attn_A_log_{d}"]), (P, NH)))
        m[f"dvec_{d}"] = np.ascontiguousarray(
            np.broadcast_to(np.asarray(ins[f"L0_attn_D_{d}"]), (P, NH)))
        m[f"normw_{d}"] = np.ascontiguousarray(
            np.asarray(ins[f"L0_attn_norm_w_{d}"]).reshape(16, P).T)
    m["w1T"] = np.ascontiguousarray(np.asarray(ins["L0_ffn_w1"]).T)
    m["w2T"] = np.ascontiguousarray(np.asarray(ins["L0_ffn_w2"]).T)
    fw = np.asarray(ins["L0_ffn_conv_w"])[:, 0, :]
    m["fconvw"] = np.ascontiguousarray(
        fw.reshape(32, P, 3).transpose(1, 0, 2).reshape(P, 96))
    m["fconvb"] = np.ascontiguousarray(
        np.asarray(ins["L0_ffn_conv_b"]).reshape(32, P).T)
    for k, v in HC.items():
        m["c_" + k] = v
    return m


def kernel(**inputs):
    if "nc" not in _NC_CACHE:
        _NC_CACHE["nc"] = build_program()
    nc = _NC_CACHE["nc"]
    in_maps = [_layout_inputs(inputs, b) for b in range(8)]
    res = run_bass_kernel_spmd(nc, in_maps, core_ids=list(range(8)))
    return np.stack([np.asarray(res.results[b]["out"]) for b in range(8)], axis=0)


# version 10 (final)
# speedup vs baseline: 1.0932x; 1.0932x over previous
import sys
sys.path.insert(0, "/opt/trn_rl_repo")
import contextlib
import numpy as np
import ml_dtypes

import concourse.bass as bass
import concourse.mybir as mybir
from concourse.bass import AP
from concourse.bass_utils import run_bass_kernel_spmd

F32 = mybir.dt.float32
BF16 = mybir.dt.bfloat16
I32 = mybir.dt.int32
AF = mybir.ActivationFunctionType
ALU = mybir.AluOpType
AXX = mybir.AxisListType.X
NPBF = ml_dtypes.bfloat16

P = 128
L = 4096
DM = 1024
DI = 2048
NH = 32
HD = 64
CD = 2304
DIP = 4384
Q = 128
NST = L // Q
FST = 256
NFST = L // FST
DH = 4096
NEGBIG = -1.0e5

_DBG = {}

class _StopRec(Exception):
    pass


class Buf:
    __slots__ = ("name", "writer", "readers")
    def __init__(self, name):
        self.name = name
        self.writer = None
        self.readers = []


class Op:
    __slots__ = ("idx", "engine", "fn", "deps", "token", "needs_inc", "is_dma",
                 "selfdep")
    def __init__(self, idx, engine, fn, is_dma):
        self.idx = idx
        self.engine = engine
        self.fn = fn
        self.deps = []
        self.token = None
        self.needs_inc = False
        self.is_dma = is_dma
        self.selfdep = False


class Prog:
    def __init__(self):
        self.ops = []
        self.bufs = {}

    def buf(self, name):
        if name not in self.bufs:
            self.bufs[name] = Buf(name)
        return self.bufs[name]

    def add(self, engine, fn, reads, writes, is_dma=False, token=None,
            selfdep=False):
        op = Op(len(self.ops), engine, fn, is_dma)
        op.token = token
        op.selfdep = selfdep
        rbufs = [self.buf(n) for n in reads]
        wbufs = [self.buf(n) for n in writes]
        for b in rbufs:
            if b.writer is not None:
                op.deps.append(b.writer)
        for b in wbufs:
            if b.writer is not None:
                op.deps.append(b.writer)
            op.deps.extend(b.readers)
        for b in wbufs:
            b.writer = op
            b.readers = []
        for b in rbufs:
            b.readers.append(op)
        self.ops.append(op)
        return op


def host_consts():
    r = np.arange(Q)
    c = {}
    for d, fwd in (("f", True), ("b", False)):
        tri_incl = (r[:, None] <= r[None, :]) if fwd else (r[:, None] >= r[None, :])
        tri_strict = (r[:, None] > r[None, :]) if fwd else (r[:, None] < r[None, :])
        mask_l = (r[:, None] < r[None, :]) if fwd else (r[:, None] > r[None, :])
        c[f"tri_incl_{d}"] = tri_incl.astype(NPBF)
        c[f"tri_strict_{d}"] = tri_strict.astype(NPBF)
    c["ident16"] = np.eye(P, dtype=NPBF)
    c["ident32"] = np.eye(P, dtype=np.float32)
    c["ones_bf"] = np.ones((P, P), dtype=NPBF)
    c["idneg"] = (np.eye(P, dtype=np.float32) * NEGBIG).astype(NPBF)
    c["cmisc"] = np.broadcast_to(
        np.array([1.0, 1e-5, -1.0, -0.5, 0.5, 0.7], np.float32), (P, 6)).copy()
    return c



def dst_half(dst, hh2):
    return AP(dst.tensor, dst.offset + hh2 * 1024,
              [list(dst.ap[0]), [1, 1024]])

def build_program():
    nc = bass.Bass("TRN2", target_bir_lowering=False,
                   detect_race_conditions=False)
    pr = Prog()
    es = contextlib.ExitStack()

    D = {}
    def din(name, shape, dtype=F32):
        D[name] = nc.dram_tensor(name, list(shape), dtype, kind="ExternalInput").ap()
    din("u", (L, DM))
    for d in ("f", "b"):
        din(f"wiT_{d}", (DM, DIP))
        din(f"woT_{d}", (DI, DM))
        din(f"convw_{d}", (P, 72))
        din(f"convb_{d}", (P, 18))
        din(f"dtb_{d}", (P, NH))
        din(f"alog_{d}", (P, NH))
        din(f"dvec_{d}", (P, NH))
        din(f"normw_{d}", (P, 16))
    din("w1T", (DM, DH))
    din("w2T", (DH, DM))
    din("fconvw", (P, 96))
    din("fconvb", (P, 32))
    HC = host_consts()
    for k, v in HC.items():
        din("c_" + k, v.shape, BF16 if v.dtype == NPBF else F32)
    out_d = nc.dram_tensor("out", [L, DM], F32, kind="ExternalOutput").ap()
    outf_d = nc.dram_tensor("scr_outf", [L, DM], F32, kind="Internal").ap()
    x1_d = nc.dram_tensor("scr_x1", [L, DM], F32, kind="Internal").ap()
    w1q_d = nc.dram_tensor("scr_w1q", [DM, DH], BF16, kind="Internal").ap()
    w2q_d = nc.dram_tensor("scr_w2q", [DH, DM], BF16, kind="Internal").ap()
    dbg_tensors = {}

    TI = {}
    def sb(name, shape, dtype):
        TI[name] = es.enter_context(nc.sbuf_tensor(name, shape, dtype))
        return TView(TI[name], 0, shape[1])

    class TView:
        __slots__ = ("t", "col0", "ncols")
        def __init__(self, t, col0, ncols):
            self.t = t; self.col0 = col0; self.ncols = ncols
        def __getitem__(self, idx):
            if idx == slice(None):
                return self.t[:, self.col0:self.col0 + self.ncols]
            if isinstance(idx, tuple):
                ps, cs = idx
                c0 = self.col0 + (cs.start or 0)
                c1 = self.col0 + (cs.stop if cs.stop is not None else self.ncols)
                return self.t[ps, c0:c1]
            raise KeyError(idx)
        def view(self, col0, ncols):
            return TView(self.t, self.col0 + col0, ncols)

    globals()["TView"] = TView

    def sb(name, shape, dtype):
        TI[name] = es.enter_context(nc.sbuf_tensor(name, shape, dtype))
        return TView(TI[name], 0, shape[1])

    wbig = sb("wbig", [P, 8 * DIP], BF16)
    wsec = sb("wsec", [P, 32 * DM], BF16)
    h1t = sb("h1t", [P, 32 * (FST + 2)], BF16)
    h2t = sb("h2t", [P, 32 * FST], BF16)
    h0 = sb("h0", [P, DI], F32)
    h0b = sb("h0b", [P, DI], BF16)
    uu = sb("uu", [P, DI], BF16)
    uf32 = sb("uf32", [P, DM], F32)
    comb = sb("comb", [P, DM], F32)
    wstage = sb("wstage", [P, 2048], F32)
    g1t = sb("g1t", [P, 1024], BF16)
    g2t = sb("g2t", [P, 1024], BF16)
    iqc = sb("iqc", [P, 256], I32)
    u16 = uu.view(0, DM)
    uT = uu.view(DM, DM)
    zsil = h1t.view(0, DI)
    xprime = h1t.view(DI, DI)
    xpp = h1t.view(2 * DI, DI)
    xdt = h1t.view(3 * DI, DI)
    xTs = h2t.view(0, DI)
    y0t = h2t.view(DI, DI)
    t1t = h2t.view(2 * DI, DI)
    ynt = h2t.view(3 * DI, DI)
    wg1 = wsec.view(16384, 2048)
    wg2 = wsec.view(16384 + 2048, 2048)
    rhsa = wsec.view(16384 + 4096, 2048)
    yTt = wsec.view(16384 + 6144, 2048)
    xbcc = wsec.view(16384 + 8192, 18 * Q)
    xbcpre = wsec.view(16384 + 8192 + 2304, 18 * (Q + 3))
    xbctm = wsec.view(16384 + 8192 + 2304 + 2358, CD)
    btok = wsec.view(16384 + 8192 + 2304 + 2358 + 2304, Q)
    x1t = h0.view(0, DI)
    q1b = h0b.view(0, DI)
    q1hb = g2t.view(0, DM)
    q1T = wbig.view(32768, 8 * (FST + 1))
    sgq = uu.view(DM, DM)
    macc = uu.view(0, FST)
    mincc = uu.view(FST, FST)
    sgnc = uu.view(2 * FST, FST)
    tqc = wstage.view(0, FST)
    s2B = wstage.view(1280, FST)
    s2rep = wstage.view(1536, P)
    qstage = uf32.view(0, 512)
    ftmp = wstage.view(1664, P)
    mrowT = wstage.view(1792, P)
    dtraw = sb("dtraw", [P, NH], F32)
    dtrawB = sb("dtrawB", [P, NH], F32)
    sm1 = sb("sm1", [P, NH], F32)
    sm2 = sb("sm2", [P, NH], F32)
    sm3 = sb("sm3", [P, NH], F32)
    dtf = sb("dtf", [P, NH], F32)
    a16t = sb("a16t", [P, NH], BF16)
    p16 = sb("p16", [P, NH], BF16)
    expac = sb("expac", [P, NH], F32)
    dtrv = sb("dtrv", [P, NH], F32)
    cdb = sb("cdb", [P, NH], F32)
    stat = sb("stat", [P, 8], F32)
    aexp = {d: sb(f"aexp{d}", [P, NH], F32) for d in ("f", "b")}
    dtbS = {d: sb(f"dtbS{d}", [P, NH], F32) for d in ("f", "b")}
    dvecS = {d: sb(f"dvecS{d}", [P, NH], F32) for d in ("f", "b")}
    cwS = {d: sb(f"cwS{d}", [P, 72], F32) for d in ("f", "b")}
    cbS = {d: sb(f"cbS{d}", [P, 18], F32) for d in ("f", "b")}
    nwS = {d: sb(f"nwS{d}", [P, 16], F32) for d in ("f", "b")}
    CT = {}
    for k, v in HC.items():
        CT[k] = sb("k_" + k, list(v.shape), BF16 if v.dtype == NPBF else F32)
    beta = sb("beta", [P, 1], F32)
    zcol = sb("zcol", [P, 1], F32)
    nwh = sb("nwh", [P, 16], F32)
    i32s = sb("i32s", [P, 1], I32)
    s2all = sb("s2all", [P, 6], F32)
    s2c = s2all.view(0, 2)
    s2cl = s2all.view(2, 2)
    f2c = s2all.view(4, 2)
    fcwS = sb("fcwS", [P, 96], F32)
    fcbS = sb("fcbS", [P, 32], F32)
    msum = sb("msum", [P, 64], F32)
    mtot = sb("mtot", [P, 4], F32)
    sq1 = sb("sq1", [P, 6], F32)

    PB = es.enter_context(nc.psum_tensor("PB", [P, 2048], F32))
    PX = es.enter_context(nc.psum_tensor("PX", [P, 2048], BF16))
    PA = es.enter_context(nc.psum_tensor("PA", [P, 512], F32))
    PG = es.enter_context(nc.psum_tensor("PG", [P, 512], F32))
    PB_BANKS = ["PB0", "PB1", "PB2", "PB3"]

    def pbbank(col):
        return f"PB{col // 512}"

    sems = {}
    for n in ("c_tensor", "c_vector", "c_scalar", "c_gpsimd"):
        sems[n] = es.enter_context(nc.semaphore(n))
    NPOOL = 8
    dma_state = {"sync": [], "gpsimd": []}
    for qn in ("sync", "gpsimd"):
        for i in range(NPOOL):
            name = f"dq_{qn}_{i}"
            sems[name] = es.enter_context(nc.semaphore(name))
            dma_state[qn].append({"name": name, "count": 0})
    dma_rr = {"sync": 0, "gpsimd": 0}

    def op(engine, fn, reads=(), writes=(), selfdep=False):
        return pr.add(engine, fn, reads, writes, selfdep=selfdep)

    def dma(queue, out_ap, in_ap, reads=(), writes=()):
        slot = dma_state[queue][dma_rr[queue] % NPOOL]
        dma_rr[queue] += 1
        prev = slot["count"]
        slot["count"] = prev + 16
        name = slot["name"]
        def fn(eng, out_ap=out_ap, in_ap=in_ap, prev=prev, name=name):
            if prev > 0:
                eng.wait_ge(sems[name], prev)
            return eng.dma_start(out_ap, in_ap).then_inc(sems[name], 16)
        return pr.add(queue, fn, reads, writes, is_dma=True,
                      token=(name, prev + 16))

    def dbg_point(tag, tiles):
        if _DBG.get("stop") != tag:
            return
        for nm, (tv, bufn, dt_, ncols, nparts) in tiles.items():
            t = nc.dram_tensor("dbg_" + nm, [nparts, ncols], dt_,
                               kind="ExternalOutput").ap()
            dma(SY, t, tv[0:nparts, slice(0, ncols)], [bufn], [])
        raise _StopRec()

    def E_mm(out_ap, lhsT_ap, rhs_ap, start, stop):
        return lambda e: e.matmul(out_ap, lhsT_ap, rhs_ap, start=start, stop=stop)
    def E_tr(out_ap, in_ap, ident):
        return lambda e: e.transpose(out_ap, in_ap, ident)
    def E_trg(out_ap, in_ap, ident, start, stop):
        return lambda e: e.matmul(out_ap, in_ap, ident, start=start, stop=stop,
                                  is_transpose=True)
    def E_tt(out_ap, a, b, opx):
        return lambda e: e.tensor_tensor(out_ap, a, b, opx)
    def E_ts(out_ap, in_ap, s1, s2, op0, op1=None):
        if op1 is None:
            return lambda e: e.tensor_scalar(out_ap, in_ap, s1, None, op0)
        return lambda e: e.tensor_scalar(out_ap, in_ap, s1, s2, op0, op1)
    def E_stt(out_ap, in0, sc, in1, op0, op1):
        return lambda e: e.scalar_tensor_tensor(out_ap, in0, sc, in1, op0, op1)
    def E_cp(out_ap, in_ap):
        return lambda e: e.tensor_copy(out_ap, in_ap)
    def E_act(out_ap, in_ap, func, bias=None, scale=None, accum=None):
        kw = {}
        if bias is not None: kw["bias"] = bias
        if scale is not None: kw["scale"] = scale
        if accum is not None: kw["accum_out"] = accum
        return lambda e: e.activation(out_ap, in_ap, func, **kw)
    def E_red(out_ap, in_ap, opx, absv=False):
        return lambda e: e.tensor_reduce(out_ap, in_ap, AXX, opx,
                                         apply_absolute_value=absv)
    def E_memset(ap_, v):
        return lambda e: e.memset(ap_, v)
    def E_recip(out_ap, in_ap):
        return lambda e: e.reciprocal(out_ap, in_ap)

    def bap(tile, free_dims, offset=0, pcount=P, pstart=0):
        base = tile[:]
        pstep = base.ap[0][0]
        return AP(base.tensor, base.offset + offset + pstart * pstep,
                  [[pstep, pcount]] + [list(x) for x in free_dims])

    def r3(tile_ap, a):
        return tile_ap.rearrange("p (a b) -> p a b", a=a)

    def zap(n, pc=P):
        return bap(zcol, [(0, n)], pcount=pc)

    V, S, T, G, SY = "vector", "scalar", "tensor", "gpsimd", "sync"

    for k in HC:
        dma(SY, CT[k][:], D["c_" + k], [], ["k_" + k])
    for d in ("f", "b"):
        dma(SY, cwS[d][:], D[f"convw_{d}"], [], [f"cw{d}"])
        dma(SY, cbS[d][:], D[f"convb_{d}"], [], [f"cb{d}"])
        dma(SY, dtbS[d][:], D[f"dtb_{d}"], [], [f"dtb{d}"])
        dma(SY, dvecS[d][:], D[f"dvec_{d}"], [], [f"dv{d}"])
        dma(SY, aexp[d][:], D[f"alog_{d}"], [], [f"al{d}"])
        dma(SY, nwS[d][:], D[f"normw_{d}"], [], [f"nw{d}"])
    dma(SY, fcwS[:], D["fconvw"], [], ["fcw"])
    dma(SY, fcbS[:], D["fconvb"], [], ["fcb"])
    op(G, E_memset(TI["zcol"][:], 0.0), [], ["zcol"])
    op(V, E_cp(i32s[:], CT["cmisc"][:, 5:6]), ["k_cmisc"], ["i32s"])
    op(V, E_stt(beta[:], i32s[:], -0.5, CT["cmisc"][:, 4:5], ALU.mult, ALU.add),
       ["i32s", "k_cmisc"], ["beta"], selfdep=True)
    for d in ("f", "b"):
        op(S, E_act(aexp[d][:], aexp[d][:], AF.Exp), [f"al{d}"], [f"al{d}"])

    def load_wi(d):
        dma(G, r3(wbig[:], 8), D[f"wiT_{d}"].rearrange("(j p) c -> p j c", p=P),
            [], ["wbig"])

    def load_wo(d):
        dma(G, AP(wsec[:].tensor, 0, [[32 * DM, P], [DM, 16], [1, DM]]),
            D[f"woT_{d}"].rearrange("(j p) c -> p j c", p=P), [], ["wsec"])
        op(V, E_stt(nwh[:], nwS[d][:], 0.5, zap(16), ALU.mult, ALU.add),
           [f"nw{d}", "zcol"], ["nwh"])
        for j in range(16):
            op(S, E_act(wsec[:, j * DM:(j + 1) * DM], wsec[:, j * DM:(j + 1) * DM],
                        AF.Copy, scale=nwh[:, j:j + 1]),
               ["wsec", "nwh"], ["wsec"])

    def wprep_abs(i):
        if i < 16:
            j, h = i // 2, i % 2
            src = D["w1T"][j * P:(j + 1) * P, h * 2048:(h + 1) * 2048]
            dma(SY, wstage[:], src, [], ["wstage"])
            op(S, E_act(g2t[:], wstage[:, 0:DM], AF.Abs, accum=msum[:, i:i + 1]),
               ["wstage"], ["g2t", "msum"])
            op(S, E_act(g2t[:], wstage[:, DM:2 * DM], AF.Abs,
                        accum=msum[:, 16 + i:17 + i]),
               ["wstage"], ["g2t", "msum"])
        else:
            for kk in range(2):
                jb = (i - 16) * 2 + kk
                src = D["w2T"][jb * P:(jb + 1) * P, :]
                dma(SY, wstage[:, 0:DM], src, [], ["wstage"])
                op(S, E_act(g2t[:], wstage[:, 0:DM], AF.Abs,
                            accum=msum[:, 32 + jb:33 + jb]),
                   ["wstage"], ["g2t", "msum"])

    def wprep_tern(i):
        if i < 16:
            j, h = i // 2, i % 2
            src = D["w1T"][j * P:(j + 1) * P, h * 2048:(h + 1) * 2048]
            dst = w1q_d[j * P:(j + 1) * P, h * 2048:(h + 1) * 2048]
            scol = mtot[:, 0:1]
            n = 2048
            dname = "w1q_dram"
        else:
            for kk in range(2):
                jb = (i - 16) * 2 + kk
                src = D["w2T"][jb * P:(jb + 1) * P, :]
                dst = w2q_d[jb * P:(jb + 1) * P, :]
                dma(SY, wstage[:, 0:DM], src, [], ["wstage"])
                op(V, E_ts(g1t[:, 0:DM], wstage[:, 0:DM], mtot[:, 1:2], 0.5,
                           ALU.mult, ALU.is_ge), ["wstage", "mtot"], ["g1t"])
                op(V, E_ts(g2t[:, 0:DM], wstage[:, 0:DM], mtot[:, 1:2], -0.5,
                           ALU.mult, ALU.is_le), ["wstage", "mtot"], ["g2t"])
                op(V, E_tt(g1t[:, 0:DM], g1t[:, 0:DM], g2t[:, 0:DM],
                           ALU.subtract), ["g1t", "g2t"], ["g1t"])
                dma(SY, dst, g1t[:, 0:DM], ["g1t"], ["w2q_dram"])
            return
        dma(SY, wstage[:], src, [], ["wstage"])
        for hh2 in range(2):
            slh = slice(hh2 * DM, (hh2 + 1) * DM)
            op(V, E_stt(g1t[:], wstage[:, slh], scol,
                        bap(CT["cmisc"], [(0, DM)], offset=4), ALU.mult, ALU.is_ge),
               ["wstage", "mtot", "k_cmisc"], ["g1t"])
            op(V, E_stt(g2t[:], wstage[:, slh], scol,
                        bap(CT["cmisc"], [(0, DM)], offset=3), ALU.mult, ALU.is_le),
               ["wstage", "mtot", "k_cmisc"], ["g2t"])
            op(V, E_tt(g1t[:], g1t[:], g2t[:], ALU.subtract),
               ["g1t", "g2t"], ["g1t"])
            dma(SY, dst.tensor_handle_unused if False else dst_half(dst, hh2),
                g1t[:], ["g1t"], [dname])

    def m_finalize():
        op(G, E_memset(ftmp[:], 0.0), ["wstage"], ["ftmp"])
        op(V, E_red(ftmp[:, 0:1], msum[:, 0:32], ALU.add), ["msum"], ["ftmp"])
        op(V, E_red(ftmp[:, 1:2], msum[:, 32:64], ALU.add), ["msum"], ["ftmp"])
        op(T, E_tr(PG[:, 128:256], ftmp[:], CT["ident32"][:]),
           ["ftmp", "k_ident32", "PG0"], ["PG0"])
        op(S, E_act(mrowT[0:2, slice(0, P)], PG[0:2, 128:256], AF.Copy),
           ["PG0", "wstage"], ["mrowT"])
        op(V, E_red(sq1[0:2, 0:1], mrowT[0:2, slice(0, P)], ALU.add), ["mrowT"], ["sq1"])
        NTOT = float(DM * DH)
        op(V, E_recip(sq1[:, 1:2], sq1[:, 0:1]), ["sq1"], ["sq1"], selfdep=True)
        op(V, E_ts(sq1[0:2, 2:3], sq1[0:2, 1:2], NTOT, None, ALU.mult), ["sq1"], ["sq1"], selfdep=True)
        op(V, E_ts(sq1[0:2, 3:4], sq1[0:2, 0:1], 1.0 / NTOT, None, ALU.mult),
           ["sq1"], ["sq1"], selfdep=True)
        op(V, E_stt(ftmp[0:2, slice(0, P)], bap(sq1, [(0, P)], offset=2, pcount=2),
                    1.0, zap(P, pc=2), ALU.mult, ALU.add),
           ["sq1", "wstage", "zcol"], ["ftmp"], selfdep=True)
        op(T, E_tr(PG[:, 128:256], ftmp[0:P, slice(0, P)], CT["ident32"][:]),
           ["ftmp", "k_ident32", "PG0"], ["PG0"])
        op(S, E_act(mtot[:, 0:2], PG[:, 128:130], AF.Copy), ["PG0"], ["mtot"])
        op(V, E_stt(ftmp[0:2, slice(0, P)], bap(sq1, [(0, P)], offset=3, pcount=2),
                    1.0, zap(P, pc=2), ALU.mult, ALU.add),
           ["sq1", "wstage", "zcol"], ["ftmp"], selfdep=True)
        op(T, E_tr(PG[:, 128:256], ftmp[0:P, slice(0, P)], CT["ident32"][:]),
           ["ftmp", "k_ident32", "PG0"], ["PG0"])
        op(S, E_act(mtot[:, 2:4], PG[:, 128:130], AF.Copy), ["PG0"], ["mtot"])

    def mamba_phase(d):
        fwd = d == "f"
        ident = CT["ident16"][:]
        st_order = list(range(NST)) if fwd else list(range(NST - 1, -1, -1))
        halo_off = 0 if fwd else Q
        tok_off = 3 if fwd else 0
        cp_src = Q if fwd else 0
        cp_dst = 0 if fwd else Q
        dstrict = "tri_strict_" + ("b" if d == "f" else "f")

        op(G, E_memset(h0[:], 0.0), [], ["h0"])
        op(G, E_cp(h0b[:], h0[:]), ["h0"], ["h0b"])
        op(G, E_memset(bap(xbcpre, [(Q + 3, 18), (1, 3)], offset=halo_off), 0.0),
           [], ["xbcpre"])

        def part1_nz(st, par):
            dtr, dtrn = (dtraw, "dtraw") if par == 0 else (dtrawB, "dtrawB")
            rows = slice(st * Q, (st + 1) * Q)
            dma(G, u16[:], D["u"][rows, :], [], ["u16"])
            for j in range(8):
                op(T, E_trg(PX[:, j * 128:(j + 1) * 128],
                            u16[:, j * 128:(j + 1) * 128], ident, j == 0, j == 7),
                   ["u16", "k_ident16", "PX0"], ["PX0"])
            op(S, E_act(uT[:], PX[:, 0:1024], AF.Copy), ["PX0"], ["uT"])
            for h in range(2):
                for g in range(2):
                    for kd in range(8):
                        c0 = DI + h * 1024 + g * 512
                        op(T, E_mm(PB[:, g * 512:(g + 1) * 512],
                                   uT[:, kd * Q:(kd + 1) * Q],
                                   wbig[:, kd * DIP + c0:kd * DIP + c0 + 512],
                                   kd == 0, kd == 7),
                           ["uT", "wbig", f"PB{g}"], [f"PB{g}"])
                op(S, E_act(xbctm[:, h * 1024:(h + 1) * 1024], PB[:, 0:1024],
                            AF.Copy), ["PB0", "PB1"], ["xbctm"])
            for kd in range(8):
                op(T, E_mm(PA[:, 0:256], uT[:, kd * Q:(kd + 1) * Q],
                           wbig[:, kd * DIP + 2 * DI:kd * DIP + 2 * DI + 256],
                           kd == 0, kd == 7), ["uT", "wbig", "PA0"], ["PA0"])
            op(S, E_act(xbctm[:, DI:CD], PA[:, 0:256], AF.Copy), ["PA0"], ["xbctm"])
            for kd in range(8):
                op(T, E_mm(PG[:, 0:32], uT[:, kd * Q:(kd + 1) * Q],
                           wbig[:, kd * DIP + DI + CD:kd * DIP + DI + CD + 32],
                           kd == 0, kd == 7), ["uT", "wbig", "PG0"], ["PG0"])
            op(S, E_act(dtr[:], PG[:, 0:32], AF.Copy), ["PG0"], [dtrn])

        def part1_z(st):
            for h in range(2):
                for g in range(2):
                    for kd in range(8):
                        c0 = h * 1024 + g * 512
                        op(T, E_mm(PB[:, g * 512:(g + 1) * 512],
                                   uT[:, kd * Q:(kd + 1) * Q],
                                   wbig[:, kd * DIP + c0:kd * DIP + c0 + 512],
                                   kd == 0, kd == 7),
                           ["uT", "wbig", f"PB{g}"], [f"PB{g}"])
                op(S, E_act(zsil[:, h * 1024:(h + 1) * 1024], PB[:, 0:1024],
                            AF.Silu), ["PB0", "PB1"], ["zsil"])

        def convsec(st):
            for grp, cnt in ((0, 8), (1, 8), (2, 2)):
                bank = grp % 2
                pxb = f"PX{bank}"
                for j in range(cnt):
                    jc = grp * 8 + j
                    op(T, E_trg(PX[:, bank * 1024 + j * 128:bank * 1024 + (j + 1) * 128],
                                xbctm[:, jc * 128:(jc + 1) * 128], ident,
                                j == 0, j == cnt - 1),
                       ["xbctm", "k_ident16", pxb], [pxb])
                out_ap = bap(xbcpre, [(Q + 3, cnt), (1, Q)],
                             offset=(grp * 8) * (Q + 3) + tok_off)
                op(S, E_act(out_ap, PX[:, bank * 1024:bank * 1024 + cnt * 128],
                            AF.Copy), [pxb], ["xbcpre"])
            for jc in range(18):
                nm = "xbccA" if jc < 9 else "xbccB"
                dst = xbcc[:, jc * Q:(jc + 1) * Q]
                for k in range(4):
                    widx = k if fwd else 3 - k
                    src2 = xbcpre[:, jc * (Q + 3) + k:jc * (Q + 3) + k + Q]
                    wcol = cwS[d][:, jc * 4 + widx:jc * 4 + widx + 1]
                    if k == 0:
                        fn = E_stt(dst, src2, wcol,
                                   bap(cbS[d], [(0, Q)], offset=jc),
                                   ALU.mult, ALU.add)
                    else:
                        fn = E_stt(dst, src2, wcol, dst, ALU.mult, ALU.add)
                    op(V, fn, ["xbcpre", f"cw{d}", f"cb{d}", nm], [nm])
            op(S, E_act(xbcc[:, 0:9 * Q], xbcc[:, 0:9 * Q], AF.Silu),
               ["xbccA"], ["xbccA"])
            op(S, E_act(xbcc[:, 9 * Q:18 * Q], xbcc[:, 9 * Q:18 * Q], AF.Silu),
               ["xbccB"], ["xbccB"])
            op(V, E_cp(bap(xbcpre, [(Q + 3, 18), (1, 3)], offset=cp_dst),
                       bap(xbcpre, [(Q + 3, 18), (1, 3)], offset=cp_src)),
               ["xbcpre"], ["xbcpre"])

        def rest_a(st, sti):
            dtr, dtrn = (dtraw, "dtraw") if sti % 2 == 0 else (dtrawB, "dtrawB")
            op(V, E_tt(sm1[:], dtr[:], dtbS[d][:], ALU.add),
               [dtrn, f"dtb{d}"], ["sm1"])
            op(S, E_act(sm2[:], sm1[:], AF.Abs), ["sm1"], ["sm2"])
            op(S, E_act(sm3[:], sm2[:], AF.Exp, scale=CT["cmisc"][:, 2:3]),
               ["sm2", "k_cmisc"], ["sm3"])
            op(S, E_act(sm2[:], sm3[:], AF.Ln, bias=CT["cmisc"][:, 0:1]),
               ["sm3", "k_cmisc"], ["sm2"])
            op(V, E_stt(dtf[:], sm1[:], 0.0, sm2[:], ALU.max, ALU.add),
               ["sm1", "sm2"], ["dtf"])
            op(V, E_tt(p16[:], dtf[:], aexp[d][:], ALU.mult), ["dtf", f"al{d}"], ["p16"])
            op(V, E_stt(a16t[:], dtf[:], -1.0, aexp[d][:], ALU.mult, ALU.mult),
               ["dtf", f"al{d}"], ["a16t"])
            op(T, E_mm(PA[:, 0:32], CT[f"tri_incl_{d}"][:], a16t[:], True, False),
               [f"k_tri_incl_{d}", "a16t", "PA0"], ["PA0"])
            op(T, E_mm(PA[:, 32:64], CT[f"tri_strict_{d}"][:], a16t[:], False, False),
               [f"k_tri_strict_{d}", "a16t"], ["PA0"])
            op(T, E_mm(PA[:, 64:96], CT["ones_bf"][:], a16t[:], False, True),
               ["k_ones_bf", "a16t"], ["PA0"])
            op(S, E_act(expac[:], PA[:, 0:32], AF.Exp), ["PA0"], ["expac"])
            op(S, E_act(sm3[:], PA[:, 32:64], AF.Exp), ["PA0"], ["sm3"])
            op(S, E_act(cdb[:], PA[:, 64:96], AF.Exp), ["PA0"], ["cdb"])
            op(V, E_tt(dtrv[:], dtf[:], sm3[:], ALU.mult), ["dtf", "sm3"], ["dtrv"])
            op(T, E_mm(PG[:, 128:256], xbcc[:, 16 * Q:17 * Q], xbcc[:, 17 * Q:18 * Q],
                       True, True), ["xbccB", "PG0"], ["PG0"])
            px = PX[:, 0:128]
            op(T, E_tr(px, xbcc[:, 16 * Q:17 * Q], ident),
               ["xbccB", "k_ident16", "PX0"], ["PX0"])
            op(S, E_act(btok[:], px, AF.Copy), ["PX0"], ["btok"])
            for grp in range(2):
                bank = grp % 2
                pxb = f"PX{bank}"
                for j in range(8):
                    jc = grp * 8 + j
                    nm = "xbccA" if jc < 9 else "xbccB"
                    op(T, E_trg(PX[:, bank * 1024 + j * 128:bank * 1024 + (j + 1) * 128],
                                xbcc[:, jc * Q:(jc + 1) * Q], ident, j == 0, j == 7),
                       [nm, "k_ident16", pxb], [pxb])
                op(S, E_act(xTs[:, grp * 1024:(grp + 1) * 1024],
                            PX[:, bank * 1024:bank * 1024 + 1024], AF.Copy),
                   [pxb], ["xTs"])
            dt_ap = bap(dtf, [(1, NH), (0, HD)])
            dtrv_ap = bap(dtrv, [(1, NH), (0, HD)])
            dv_ap = bap(dvecS[d], [(1, NH), (0, HD)])
            op(V, E_tt(r3(xprime[:], NH), r3(xTs[:], NH), dt_ap, ALU.mult),
               ["xTs", "dtf"], ["xprime"])
            op(V, E_tt(r3(xpp[:], NH), r3(xTs[:], NH), dtrv_ap, ALU.mult),
               ["xTs", "dtrv"], ["xpp"])
            op(V, E_tt(r3(xdt[:], NH), r3(xTs[:], NH), dv_ap, ALU.mult),
               ["xTs", f"dv{d}"], ["xdt"])
            for H, wgt, wgn in ((0, wg1, "wg1"), (1, wg2, "wg2")):
                hoff = H * 16
                op(V, E_tt(r3(rhsa[:], 16), bap(a16t, [(1, 16), (0, Q)], offset=hoff),
                           bap(CT[f"tri_incl_{d}"], [(0, 16), (1, Q)]), ALU.mult),
                   ["a16t", f"k_tri_incl_{d}"], ["rhsa"])
                for q in range(2):
                    qoff = q * 1024
                    for g in range(2):
                        sl = PB[:, 1024 + g * 512:1024 + (g + 1) * 512]
                        pbn = f"PB{2 + g}"
                        op(T, E_mm(sl, CT["ones_bf"][:],
                                   rhsa[:, qoff + g * 512:qoff + (g + 1) * 512],
                                   True, False), ["k_ones_bf", "rhsa", pbn], [pbn])
                        op(T, E_mm(sl, CT[f"tri_incl_{d}"][:],
                                   bap(p16, [(1, 4), (0, Q)],
                                       offset=hoff + q * 8 + g * 4),
                                   False, False), [f"k_tri_incl_{d}", "p16"], [pbn])
                        op(T, E_mm(sl, CT[dstrict][:],
                                   bap(CT["idneg"], [(0, 4), (1, Q)]),
                                   False, True), ["k_" + dstrict, "k_idneg"], [pbn])
                    op(S, E_act(wgt[:, qoff:qoff + 1024], PB[:, 1024:2048], AF.Exp),
                       ["PB2", "PB3"], [wgn])
                gr_ap = AP(PG[:].tensor, 128, [[512, P], [0, 16], [1, Q]])
                op(V, E_tt(r3(wgt[:], 16), r3(wgt[:], 16), gr_ap, ALU.mult),
                   [wgn, "PG0"], [wgn])
            exa = bap(expac, [(1, NH), (0, HD)])
            for Hh in range(2):
                off = Hh * 1024
                for g in range(2):
                    pbn = f"PB{2 + g}"
                    op(T, E_mm(PB[:, 1024 + g * 512:1024 + (g + 1) * 512],
                               xbcc[:, 17 * Q:18 * Q],
                               h0b[:, off + g * 512:off + (g + 1) * 512], True, True),
                       ["xbcc", "h0b", pbn], [pbn])
                exa_h = bap(expac, [(1, 16), (0, HD)], offset=Hh * 16)
                op(V, E_tt(r3(y0t[:, off:off + 1024], 16),
                           r3(PB[:, 1024:2048], 16), exa_h, ALU.mult),
                   ["PB2", "PB3", "expac"], ["y0t"])
                for g in range(2):
                    pbn = f"PB{2 + g}"
                    op(T, E_mm(PB[:, 1024 + g * 512:1024 + (g + 1) * 512],
                               CT["ident16"][:],
                               xdt[:, off + g * 512:off + (g + 1) * 512], True, False),
                       ["k_ident16", "xdt", pbn], [pbn])
                    for hh in range(8):
                        h = Hh * 16 + g * 8 + hh
                        wgt, wgn = (wg1, "wg1") if h < 16 else (wg2, "wg2")
                        op(T, E_mm(PB[:, 1024 + (g * 8 + hh) * HD:
                                       1024 + (g * 8 + hh + 1) * HD],
                                   wgt[:, (h % 16) * Q:((h % 16) + 1) * Q],
                                   xprime[:, h * HD:(h + 1) * HD], False, hh == 7),
                           [wgn, "xprime", pbn], [pbn])
                op(V, E_stt(t1t[:, off:off + 1024], PB[:, 1024:2048], 1.0,
                            y0t[:, off:off + 1024], ALU.mult, ALU.add),
                   ["PB2", "PB3", "y0t"], ["t1t"])
            op(V, E_tt(t1t[:], t1t[:], zsil[:], ALU.mult), ["t1t", "zsil"], ["t1t"])

        def rest_b(st, sti):
            rows = slice(st * Q, (st + 1) * Q)
            op(S, E_act(rhsa[:], t1t[:], AF.Square, accum=stat[:, 0:1]),
               ["t1t", "rhsa"], ["rhsa", "stat"])
            op(V, E_ts(stat[:, 1:2], stat[:, 0:1], 1.0 / DI, 1e-5, ALU.mult, ALU.add),
               ["stat"], ["stat"], selfdep=True)
            op(S, E_act(stat[:, 2:3], stat[:, 1:2], AF.Ln), ["stat"], ["stat"])
            op(S, E_act(stat[:, 3:4], stat[:, 2:3], AF.Exp, scale=CT["cmisc"][:, 3:4]),
               ["stat", "k_cmisc"], ["stat"])
            op(S, E_act(ynt[:], t1t[:], AF.Copy, scale=stat[:, 3:4]),
               ["t1t", "stat"], ["ynt"], selfdep=True)
            for grp in range(2):
                bank = grp % 2
                pxb = f"PX{bank}"
                for j in range(8):
                    jc = grp * 8 + j
                    op(T, E_trg(PX[:, bank * 1024 + j * 128:bank * 1024 + (j + 1) * 128],
                                ynt[:, jc * 128:(jc + 1) * 128], ident, j == 0, j == 7),
                       ["ynt", "k_ident16", pxb], [pxb])
                op(S, E_act(yTt[:, grp * 1024:(grp + 1) * 1024],
                            PX[:, bank * 1024:bank * 1024 + 1024], AF.Copy),
                   [pxb], ["yTt"])
            for Hh in range(2):
                off = Hh * 1024
                for g in range(2):
                    pbn = f"PB{2 + g}"
                    op(T, E_mm(PB[:, 1024 + g * 512:1024 + (g + 1) * 512], btok[:],
                               xpp[:, off + g * 512:off + (g + 1) * 512], True, True),
                       ["btok", "xpp", pbn], [pbn])
                for h in range(16):
                    hh = Hh * 16 + h
                    op(V, E_stt(h0[:, hh * HD:(hh + 1) * HD],
                                h0[:, hh * HD:(hh + 1) * HD],
                                cdb[:, hh:hh + 1], PB[:, 1024 + h * HD:1024 + (h + 1) * HD],
                                ALU.mult, ALU.add),
                       ["h0", "cdb", f"PB{2 + (h * HD * 4) // 2048}"], ["h0"])
            op(G, E_cp(h0b[:], h0[:]), ["h0"], ["h0b"])
            for g in range(2):
                pbn = f"PB{2 + g}"
                for kc in range(16):
                    op(T, E_mm(PB[:, 1024 + g * 512:1024 + (g + 1) * 512],
                               yTt[:, kc * Q:(kc + 1) * Q],
                               wsec[:, kc * DM + g * 512:kc * DM + (g + 1) * 512],
                               kc == 0, kc == 15),
                       ["yTt", "wsec", pbn], [pbn])
            pb23 = ["PB2", "PB3"]
            if d == "f":
                op(V, E_cp(comb[:], PB[:, 1024:2048]), pb23, ["comb"])
                dma(SY, outf_d[rows, :], comb[:], ["comb"], ["outf_dram"])
            else:
                dma(SY, comb[:], outf_d[rows, :], ["outf_dram"], ["comb"])
                dma(SY, uf32[:], D["u"][rows, :], [], ["uf32"])
                op(V, E_tt(comb[:], PB[:, 1024:2048], comb[:], ALU.add),
                   pb23 + ["comb"], ["comb"])
                op(V, E_tt(comb[:], comb[:], uf32[:], ALU.add),
                   ["comb", "uf32"], ["comb"])
                op(V, E_red(stat[:, 0:1], comb[:], ALU.add), ["comb"], ["stat"])
                op(V, E_ts(stat[:, 1:2], stat[:, 0:1], 1.0 / DM, None, ALU.mult),
                   ["stat"], ["stat"], selfdep=True)
                op(S, E_act(sgq[:], comb[:], AF.Square, accum=stat[:, 2:3]),
                   ["comb", "sgq"], ["sgq", "stat"])
                op(V, E_tt(stat[:, 3:4], stat[:, 1:2], stat[:, 1:2], ALU.mult),
                   ["stat"], ["stat"], selfdep=True)
                op(V, E_stt(stat[:, 4:5], stat[:, 2:3], 1.0 / DM, stat[:, 3:4],
                            ALU.mult, ALU.subtract), ["stat"], ["stat"], selfdep=True)
                op(S, E_act(stat[:, 5:6], stat[:, 4:5], AF.Ln, bias=CT["cmisc"][:, 1:2]),
                   ["stat", "k_cmisc"], ["stat"])
                op(S, E_act(stat[:, 6:7], stat[:, 5:6], AF.Exp, scale=CT["cmisc"][:, 3:4]),
                   ["stat", "k_cmisc"], ["stat"])
                op(V, E_stt(stat[:, 7:8], stat[:, 1:2], -1.0, stat[:, 6:7],
                            ALU.mult, ALU.mult), ["stat"], ["stat"], selfdep=True)
                op(S, E_act(comb[:], comb[:], AF.Identity, scale=stat[:, 6:7],
                            bias=stat[:, 7:8]), ["comb", "stat"], ["comb"],
                   selfdep=True)
                dma(SY, x1_d[rows, :], comb[:], ["comb"], ["x1_dram"])

        part1_nz(st_order[0], 0)
        part1_z(st_order[0])
        convsec(st_order[0])
        for sti, st in enumerate(st_order):
            if d == "f":
                wprep_abs(sti)
            else:
                wprep_tern(sti)
            if sti + 1 < NST:
                part1_nz(st_order[sti + 1], (sti + 1) % 2)
            rest_a(st, sti)
            if sti + 1 < NST:
                part1_z(st_order[sti + 1])
            rest_b(st, sti)
            if sti + 1 < NST:
                convsec(st_order[sti + 1])

    def ffn_phase():
        ident = CT["ident16"][:]
        dma(SY, r3(wbig[:, 0:8 * DH], 8), w1q_d.rearrange("(j p) c -> p j c", p=P),
            ["w1q_dram"], ["wbig"])
        dma(SY, AP(wsec[:].tensor, 0, [[32 * DM, P], [DM, 32], [1, DM]]),
            w2q_d.rearrange("(j p) c -> p j c", p=P), ["w2q_dram"],
            ["wsec", "wg1", "wg2", "rhsa", "yTt", "xbccA", "xbccB", "xbcpre", "xbctm", "btok"])
        op(G, E_memset(h1t[:], 0.0),
           [], ["h1t", "zsil", "xprime", "xpp", "xdt"])
        op(G, E_memset(h2t[:, 0:4], 0.0),
           [], ["h2t", "xTs", "y0t", "t1t", "ynt"])
        op(G, E_memset(uu[:, 0:4], 0.0),
           [], ["u16", "uT", "macc", "mincc", "sgnc", "sgq"])
        op(G, E_memset(h0[:, 0:4], 0.0), [], ["h0", "x1t"])
        op(G, E_memset(uf32[:, 0:4], 0.0), [], ["uf32", "qstage"])
        op(G, E_memset(g1t[:, 0:4], 0.0), [], ["g1t"])
        op(G, E_memset(h0b[:, 0:4], 0.0), [], ["h0b", "q1b"])
        op(G, E_memset(g2t[:, 0:4], 0.0), [], ["g2t", "q1hb"])
        op(G, E_memset(wstage[:, 0:4], 0.0), [], ["wstage", "tqc", "s2B", "s2rep", "x1h"])
        op(G, E_memset(q1hb[:], 0.0), ["q1hb"], ["q1hb"])

        for st in range(NFST):
            dma(SY, x1t[:, 0:DM], x1_d[st * FST:st * FST + P, :],
                ["x1_dram"], ["x1t"])
            dma(SY, x1t[:, DM:2 * DM], x1_d[st * FST + P:st * FST + 2 * P, :],
                ["x1_dram"], ["x1t"])
            has_halo = st < NFST - 1
            if has_halo:
                dma(SY, wstage[0:1, 256:1280], x1_d[(st + 1) * FST:(st + 1) * FST + 1, :],
                    ["x1_dram"], ["x1h"])
            for b in range(2):
                xb = x1t[:, b * DM:(b + 1) * DM]
                op(V, E_red(sq1[:, 0:1], xb, ALU.max, absv=True), ["x1t"], ["sq1"])
                op(V, E_ts(sq1[:, 1:2], sq1[:, 0:1], 1e-5, None, ALU.max),
                   ["sq1"], ["sq1"], selfdep=True)
                op(V, E_recip(sq1[:, 2:3], sq1[:, 1:2]), ["sq1"], ["sq1"], selfdep=True)
                op(V, E_ts(sq1[:, 3:4], sq1[:, 2:3], 127.0, None, ALU.mult),
                   ["sq1"], ["sq1"], selfdep=True)
                op(V, E_ts(sq1[:, 4:5], sq1[:, 1:2], 1.0 / 127.0, None, ALU.mult),
                   ["sq1"], ["sq1"], selfdep=True)
                op(V, E_tt(sq1[:, 5:6], sq1[:, 4:5], mtot[:, 2:3], ALU.mult),
                   ["sq1", "mtot"], ["sq1"], selfdep=True)
                op(S, E_act(sgq[:], xb, AF.Sign), ["x1t"], ["sgq"])
                for hp in range(4):
                    sl = slice(hp * 256, (hp + 1) * 256)
                    op(S, E_act(qstage[:, slice(0, 256)], xb[:, sl], AF.Copy,
                                scale=sq1[:, 3:4]), ["x1t", "sq1"], ["qstage"])
                    op(V, E_stt(iqc[:], sgq[:, sl], beta[:], qstage[:, slice(0, 256)],
                                ALU.mult, ALU.add), ["sgq", "beta", "qstage"], ["iqc"])
                    op(V, E_stt(q1b[:, b * DM + hp * 256:b * DM + (hp + 1) * 256],
                                iqc[:], sq1[:, 5:6], zap(256), ALU.mult, ALU.add),
                       ["iqc", "zcol", "sq1"], ["q1b"])
            if has_halo:
                op(V, E_red(sq1[0:1, 0:1], wstage[0:1, 256:1280], ALU.max, absv=True),
                   ["x1h"], ["sq1"])
                op(V, E_ts(sq1[0:1, 1:2], sq1[0:1, 0:1], 1e-5, None, ALU.max),
                   ["sq1"], ["sq1"], selfdep=True)
                op(V, E_recip(sq1[0:1, 2:3], sq1[0:1, 1:2]), ["sq1"], ["sq1"], selfdep=True)
                op(V, E_ts(sq1[0:1, 3:4], sq1[0:1, 2:3], 127.0, None, ALU.mult),
                   ["sq1"], ["sq1"], selfdep=True)
                op(V, E_ts(sq1[0:1, 4:5], sq1[0:1, 1:2], 1.0 / 127.0, None, ALU.mult),
                   ["sq1"], ["sq1"], selfdep=True)
                op(V, E_tt(sq1[0:1, 5:6], sq1[0:1, 4:5], mtot[0:1, 2:3], ALU.mult),
                   ["sq1", "mtot"], ["sq1"], selfdep=True)
                op(S, E_act(sgq[0:1, slice(0, DM)], wstage[0:1, 256:1280], AF.Sign), ["x1h"], ["sgq"])
                for hp in range(4):
                    sl = slice(hp * 256, (hp + 1) * 256)
                    op(S, E_act(qstage[0:1, slice(0, 256)],
                                wstage[0:1, 256 + hp * 256:256 + (hp + 1) * 256],
                                AF.Copy, scale=sq1[0:1, 3:4]), ["x1h", "sq1"], ["qstage"])
                    op(V, E_stt(iqc[0:1, slice(0, 256)], sgq[0:1, sl], beta[0:1, 0:1],
                                qstage[0:1, slice(0, 256)], ALU.mult, ALU.add),
                       ["sgq", "beta", "qstage"], ["iqc"])
                    op(V, E_stt(q1hb[0:1, sl], iqc[0:1, slice(0, 256)], sq1[0:1, 5:6],
                                zap(256, pc=1), ALU.mult, ALU.add),
                       ["iqc", "zcol", "sq1"], ["q1hb"])
            else:
                op(G, E_memset(q1hb[0:1, :], 0.0), [], ["q1hb"])
            for grp in range(2):
                bank = grp
                pxb = f"PX{bank}"
                for j in range(8):
                    kd, b = (grp * 4 + j // 2), j % 2
                    op(T, E_trg(PX[:, bank * 1024 + j * 128:bank * 1024 + (j + 1) * 128],
                                q1b[:, b * DM + kd * 128:b * DM + (kd + 1) * 128],
                                ident, j == 0, j == 7),
                       ["q1b", "k_ident16", pxb], [pxb])
                out_ap = AP(q1T[:].tensor, q1T[:].offset + (grp * 4) * (FST + 1),
                            [[q1T[:].ap[0][0], P], [FST + 1, 4], [P, 2], [1, P]])
                op(S, E_act(out_ap, PX[:, bank * 1024:bank * 1024 + 1024], AF.Copy),
                   [pxb], ["q1T", "wbig"])
            for grp in range(2):
                bank = grp
                pxb = f"PX{bank}"
                for j in range(4):
                    kd = grp * 4 + j
                    op(T, E_trg(PX[:, bank * 1024 + j * 128:bank * 1024 + (j + 1) * 128],
                                q1hb[:, kd * 128:(kd + 1) * 128], ident,
                                j == 0, j == 3),
                       ["q1hb", "k_ident16", pxb], [pxb])
                in_ap = AP(PX[:].tensor, bank * 1024, [[2048, P], [128, 4], [1, 1]])
                out_ap = AP(q1T[:].tensor, q1T[:].offset + (grp * 4) * (FST + 1) + FST,
                            [[q1T[:].ap[0][0], P], [FST + 1, 4], [1, 1]])
                op(S, E_act(out_ap, in_ap, AF.Copy), [pxb], ["q1T", "wbig"])
            if st == _DBG.get("fst", -1):
                dbg_point("q1", {
                    "q1b": (q1b, "q1b", BF16, DI, P),
                    "q1T": (q1T, "q1T", BF16, 8 * (FST + 1), P),
                    "x1t": (x1t, "x1t", F32, DI, P),
                    "sgq": (sgq, "sgq", BF16, DM, P),
                    "sq1": (sq1, "sq1", F32, 4, P),
                    "beta": (beta, "beta", F32, 1, P),
                    "iqc": (iqc, "iqc", I32, 256, P),
                    "qstage": (qstage, "qstage", F32, 512, P),
                    "i32s": (i32s, "i32s", I32, 1, P),
                    "cmisc": (CT["cmisc"], "k_cmisc", F32, 6, P)})
            op(V, E_cp(bap(h1t, [(FST + 2, 32), (1, 1)], offset=0),
                       bap(h1t, [(FST + 2, 32), (1, 1)], offset=FST)),
               ["h1t"], ["h1t"])
            for jc in range(32):
                ptile, pn = (PA, "PA0") if jc % 2 == 0 else (PG, "PG0")
                for kd in range(8):
                    op(T, E_mm(ptile[:, 0:FST + 1],
                               wbig[:, kd * DH + jc * 128:kd * DH + (jc + 1) * 128],
                               q1T[:, kd * (FST + 1):(kd + 1) * (FST + 1)],
                               kd == 0, kd == 7),
                       ["wbig", "q1T", pn], [pn])
                op(S, E_act(h1t[:, jc * (FST + 2) + 1:jc * (FST + 2) + FST + 2],
                            ptile[:, 0:FST + 1], AF.Copy), [pn], ["h1t"])
            if st == _DBG.get("fst", -1):
                dbg_point("h1", {"h1t": (h1t, "h1t", BF16, 32 * (FST + 2), P),
                                 "wq0": (wbig.view(0, 4096), "wbig", BF16, 4096, P),
                                 "q1Tb": (q1T, "q1T", BF16, 8 * (FST + 1), P)})
            for jc in range(32):
                dst = h2t[:, jc * FST:(jc + 1) * FST]
                for k in range(3):
                    src = h1t[:, jc * (FST + 2) + k:jc * (FST + 2) + k + FST]
                    if k == 0:
                        fn = E_stt(dst, src, fcwS[:, jc * 3:jc * 3 + 1],
                                   bap(fcbS, [(0, FST)], offset=jc),
                                   ALU.mult, ALU.add)
                    else:
                        fn = E_stt(dst, src, fcwS[:, jc * 3 + k:jc * 3 + k + 1],
                                   dst, ALU.mult, ALU.add)
                    op(V, fn, ["h1t", "fcw", "fcb", "h2t"], ["h2t"])
            op(S, E_act(h2t[:], h2t[:], AF.Silu), ["h2t"], ["h2t"])
            op(V, E_stt(macc[:], h2t[:, 0:FST], 1.0, h2t[:, 0:FST],
                        ALU.mult, ALU.max), ["h2t"], ["macc"])
            op(V, E_stt(mincc[:], h2t[:, 0:FST], 1.0, h2t[:, 0:FST],
                        ALU.mult, ALU.min), ["h2t"], ["mincc"])
            for jc in range(1, 32):
                sl = h2t[:, jc * FST:(jc + 1) * FST]
                op(V, E_tt(macc[:], macc[:], sl, ALU.max), ["h2t", "macc"], ["macc"])
                op(V, E_tt(mincc[:], mincc[:], sl, ALU.min), ["h2t", "mincc"], ["mincc"])
            op(V, E_stt(macc[:], mincc[:], -1.0, macc[:], ALU.mult, ALU.max),
               ["macc", "mincc"], ["macc"])
            for b in range(2):
                slot = (b % 2) * 1024
                px = PX[:, slot:slot + 128]
                pxb = f"PX{b % 2}"
                op(T, E_tr(px, macc[:, b * 128:(b + 1) * 128], ident),
                   ["macc", "k_ident16", pxb], [pxb])
                op(V, E_red(s2c[:, b:b + 1], px, ALU.max), [pxb], ["s2c"])
            op(V, E_stt(s2cl[:], s2c[:], 1.0, bap(CT["cmisc"], [(0, 2)], offset=1),
                        ALU.mult, ALU.max), ["s2c", "k_cmisc"], ["s2cl"], selfdep=True)
            for bb in range(2):
                op(V, E_recip(f2c[:, bb:bb + 1], s2cl[:, bb:bb + 1]), ["s2cl"], ["f2c"], selfdep=True)
            op(V, E_stt(s2c[:], f2c[:], 127.0, zap(2), ALU.mult, ALU.add),
               ["f2c", "zcol"], ["s2c"], selfdep=True)
            op(V, E_stt(f2c[:], s2cl[:], 1.0 / 127.0, zap(2), ALU.mult, ALU.add),
               ["s2cl", "zcol"], ["f2c"], selfdep=True)
            op(V, E_tt(f2c[:], f2c[:], bap(mtot, [(0, 2)], offset=3), ALU.mult),
               ["f2c", "mtot"], ["f2c"], selfdep=True)
            for b in range(2):
                op(V, E_stt(s2rep[:], bap(s2c, [(0, P)], offset=b), 1.0, zap(P),
                            ALU.mult, ALU.add), ["s2c", "zcol"], ["s2rep"], selfdep=True)
                op(T, E_tr(PG[:, 128:256], s2rep[:], CT["ident32"][:]),
                   ["s2rep", "k_ident32", "PG0"], ["PG0"])
                op(S, E_act(s2B[:, b * 128:(b + 1) * 128], PG[:, 128:256], AF.Copy),
                   ["PG0"], ["s2B"])
            for jc in range(32):
                sl = h2t[:, jc * FST:(jc + 1) * FST]
                op(V, E_tt(tqc[:], sl, s2B[:], ALU.mult), ["h2t", "s2B"], ["tqc"])
                op(S, E_act(sgnc[:], sl, AF.Sign), ["h2t"], ["sgnc"])
                op(V, E_stt(iqc[:], sgnc[:], beta[:], tqc[:],
                            ALU.mult, ALU.add), ["sgnc", "beta", "tqc"], ["iqc"])
                op(V, E_stt(sl, iqc[:], 1.0, zap(FST), ALU.mult, ALU.add),
                   ["iqc", "zcol"], ["h2t"])
            if st == _DBG.get("fst", -1):
                dbg_point("h2", {"h2t": (h2t, "h2t", BF16, 32 * FST, P),
                                  "s2B": (s2B, "s2B", F32, FST, P),
                                  "f2c": (f2c, "f2c", F32, 2, P),
                                  "mtot": (mtot, "mtot", F32, 4, P)})
            for b in range(2):
                for half in range(2):
                    for jc in range(32):
                        op(T, E_mm(PB[:, half * 512:(half + 1) * 512],
                                   h2t[:, jc * FST + b * P:jc * FST + (b + 1) * P],
                                   wsec[:, jc * DM + half * 512:jc * DM + (half + 1) * 512],
                                   jc == 0, jc == 31),
                           ["h2t", "wsec", PB_BANKS[half]], [PB_BANKS[half]])
                pb01 = ["PB0", "PB1"]
                op(V, E_stt(comb[:], PB[:, 0:1024], f2c[:, b:b + 1],
                            x1t[:, b * DM:(b + 1) * DM], ALU.mult, ALU.add),
                   pb01 + ["f2c", "x1t"], ["comb"])
                op(V, E_red(stat[:, 0:1], comb[:], ALU.add), ["comb"], ["stat"])
                op(V, E_ts(stat[:, 1:2], stat[:, 0:1], 1.0 / DM, None, ALU.mult),
                   ["stat"], ["stat"], selfdep=True)
                op(S, E_act(sgq[:], comb[:], AF.Square, accum=stat[:, 2:3]),
                   ["comb", "sgq"], ["sgq", "stat"])
                op(V, E_tt(stat[:, 3:4], stat[:, 1:2], stat[:, 1:2], ALU.mult),
                   ["stat"], ["stat"], selfdep=True)
                op(V, E_stt(stat[:, 4:5], stat[:, 2:3], 1.0 / DM, stat[:, 3:4],
                            ALU.mult, ALU.subtract), ["stat"], ["stat"], selfdep=True)
                op(S, E_act(stat[:, 5:6], stat[:, 4:5], AF.Ln, bias=CT["cmisc"][:, 1:2]),
                   ["stat", "k_cmisc"], ["stat"])
                op(S, E_act(stat[:, 6:7], stat[:, 5:6], AF.Exp, scale=CT["cmisc"][:, 3:4]),
                   ["stat", "k_cmisc"], ["stat"])
                op(V, E_stt(stat[:, 7:8], stat[:, 1:2], -1.0, stat[:, 6:7],
                            ALU.mult, ALU.mult), ["stat"], ["stat"], selfdep=True)
                op(S, E_act(comb[:], comb[:], AF.Identity, scale=stat[:, 6:7],
                            bias=stat[:, 7:8]), ["comb", "stat"], ["comb"],
                   selfdep=True)
                dma(SY, out_d[st * FST + b * P:st * FST + (b + 1) * P, :], comb[:],
                    ["comb"], [])

    try:
        load_wi("f")
        load_wo("f")
        mamba_phase("f")
        if _DBG.get("stop") == "endF":
            t = nc.dram_tensor("dbg_outf", [L, DM], F32, kind="ExternalOutput").ap()
            dma(SY, t, outf_d, ["outf_dram"], [])
            raise _StopRec()
        m_finalize()
        load_wi("b")
        load_wo("b")
        mamba_phase("b")
        if _DBG.get("stop") == "endB":
            t = nc.dram_tensor("dbg_x1", [L, DM], F32, kind="ExternalOutput").ap()
            dma(SY, t, x1_d, ["x1_dram"], [])
            t2 = nc.dram_tensor("dbg_mtot", [P, 4], F32, kind="ExternalOutput").ap()
            dma(SY, t2, mtot[:], ["mtot"], [])
            raise _StopRec()
        ffn_phase()
    except _StopRec:
        pass

    for o in pr.ops:
        o.deps = [dd for dd in o.deps
                  if dd.engine != o.engine or dd.is_dma or o.selfdep]
        for dd in o.deps:
            dd.needs_inc = True
    counters = {}
    for o in pr.ops:
        if o.is_dma:
            continue
        if o.needs_inc:
            semn = "c_" + o.engine
            counters[semn] = counters.get(semn, 0) + 1
            o.token = (semn, counters[semn])

    eng_ops = {e: [] for e in ("tensor", "vector", "scalar", "gpsimd", "sync")}
    for o in pr.ops:
        eng_ops[o.engine].append(o)
    watermark = {e: {} for e in eng_ops}

    def emit(engine, eng):
        wm = watermark[engine]
        for o in eng_ops[engine]:
            for dd in o.deps:
                if dd.token is None:
                    continue
                semn, val = dd.token
                if wm.get(semn, 0) >= val:
                    continue
                eng.wait_ge(sems[semn], val)
                wm[semn] = val
            inst = o.fn(eng)
            if o.needs_inc and not o.is_dma:
                inst.then_inc(sems[o.token[0]], 1)

    with nc.Block() as blk:
        @blk.tensor
        def _(e):
            emit("tensor", e)
        @blk.vector
        def _(e):
            emit("vector", e)
        @blk.scalar
        def _(e):
            emit("scalar", e)
        @blk.gpsimd
        def _(e):
            emit("gpsimd", e)
            for slot in dma_state["gpsimd"]:
                if slot["count"] > 0:
                    e.wait_ge(sems[slot["name"]], slot["count"])
        @blk.sync
        def _(e):
            emit("sync", e)
            for slot in dma_state["sync"]:
                if slot["count"] > 0:
                    e.wait_ge(sems[slot["name"]], slot["count"])

    es.close()
    return nc



_NC_CACHE = {}


def _layout_inputs(ins, b):
    HC = host_consts()
    m = {"u": np.ascontiguousarray(np.asarray(ins["L0_attn_u"])[b], dtype=np.float32)}
    for d in ("f", "b"):
        m[f"wiT_{d}"] = np.ascontiguousarray(np.asarray(ins[f"L0_attn_Wi_{d}"]).T)
        m[f"woT_{d}"] = np.ascontiguousarray(np.asarray(ins[f"L0_attn_Wo_{d}"]).T)
        cw = np.asarray(ins[f"L0_attn_conv_w_{d}"])[:, 0, :]
        m[f"convw_{d}"] = np.ascontiguousarray(
            cw.reshape(18, P, 4).transpose(1, 0, 2).reshape(P, 72))
        m[f"convb_{d}"] = np.ascontiguousarray(
            np.asarray(ins[f"L0_attn_conv_b_{d}"]).reshape(18, P).T)
        m[f"dtb_{d}"] = np.ascontiguousarray(
            np.broadcast_to(np.asarray(ins[f"L0_attn_dt_bias_{d}"]), (P, NH)))
        m[f"alog_{d}"] = np.ascontiguousarray(
            np.broadcast_to(np.asarray(ins[f"L0_attn_A_log_{d}"]), (P, NH)))
        m[f"dvec_{d}"] = np.ascontiguousarray(
            np.broadcast_to(np.asarray(ins[f"L0_attn_D_{d}"]), (P, NH)))
        m[f"normw_{d}"] = np.ascontiguousarray(
            np.asarray(ins[f"L0_attn_norm_w_{d}"]).reshape(16, P).T)
    m["w1T"] = np.ascontiguousarray(np.asarray(ins["L0_ffn_w1"]).T)
    m["w2T"] = np.ascontiguousarray(np.asarray(ins["L0_ffn_w2"]).T)
    fw = np.asarray(ins["L0_ffn_conv_w"])[:, 0, :]
    m["fconvw"] = np.ascontiguousarray(
        fw.reshape(32, P, 3).transpose(1, 0, 2).reshape(P, 96))
    m["fconvb"] = np.ascontiguousarray(
        np.asarray(ins["L0_ffn_conv_b"]).reshape(32, P).T)
    for k, v in HC.items():
        m["c_" + k] = v
    return m


def kernel(**inputs):
    if "nc" not in _NC_CACHE:
        _NC_CACHE["nc"] = build_program()
    nc = _NC_CACHE["nc"]
    in_maps = [_layout_inputs(inputs, b) for b in range(8)]
    res = run_bass_kernel_spmd(nc, in_maps, core_ids=list(range(8)))
    return np.stack([np.asarray(res.results[b]["out"]) for b in range(8)], axis=0)
